# version 09
# baseline (speedup 1.0000x reference)
import numpy as np
import concourse.bass as bass
import concourse.mybir as mybir
from concourse.bass_utils import run_bass_kernel_spmd
from contextlib import ExitStack

F32 = mybir.dt.float32
BF16 = mybir.dt.bfloat16
ALU = mybir.AluOpType
ACT = mybir.ActivationFunctionType

HID = 1024
INNER = 512
HD = 64
NB = 4
SEQ = 8192
TH = 4096
TT = 256
SUB = TT // 128
NT = TH // TT
NOC = INNER // 128
NFC = HID // 128
NSLAB = 128
WSF = 1152
EPS1 = 1e-5
EPSI = 1e-6

ENGS = ("tensor", "vector", "scalar", "gpsimd", "sync")
TILE_PHASE = 99


class Prog:

    def __init__(self):
        self.stream = {e: [] for e in ENGS}
        self.cnt = {}
        self.lastw = {e: {} for e in ENGS}
        self.writer = {}
        self.readers = {}
        self.eng_sem = {"tensor": "T", "vector": "V", "scalar": "S"}

    def _want(self, eng, waits, sem, val):
        if val <= 0:
            return
        if sem == self.eng_sem.get(eng):
            return
        if val > self.lastw[eng].get(sem, 0):
            self.lastw[eng][sem] = val
            waits.append((sem, val))

    def stage(self, eng, fn, reads=(), writes=(), inc=None, extra=()):
        waits = []
        for sem, val in extra:
            self._want(eng, waits, sem, val)
        for r in reads:
            w = self.writer.get(r)
            if w:
                self._want(eng, waits, *w)
        for r in writes:
            w = self.writer.get(r)
            if w:
                self._want(eng, waits, *w)
            for rd in self.readers.get(r, ()):
                self._want(eng, waits, *rd)
        if inc is None:
            sem, amt = self.eng_sem[eng], 1
        else:
            sem, amt = inc
        self.cnt[sem] = self.cnt.get(sem, 0) + amt
        mark = (sem, self.cnt[sem])
        for r in reads:
            self.readers.setdefault(r, []).append(mark)
        for r in writes:
            self.writer[r] = mark
            self.readers[r] = []
        self.stream[eng].append((waits, fn, (sem, amt)))
        return mark

    def barrier_wait(self, eng, marks):
        waits = []
        for sem, val in marks:
            self._want(eng, waits, sem, val)
        self.stream[eng].append((waits, None, None))


def build_nc(n_pre=NT, n_own=NT, do_cbuild=True, do_prepass=True, debug=False):
    nc = bass.Bass("TRN2", target_bir_lowering=False, debug=False,
                   detect_race_conditions=False)

    x_own = nc.dram_tensor("x_own", [TH, HID], F32, kind="ExternalInput").ap()
    x_pre = nc.dram_tensor("x_pre", [TH, HID], F32, kind="ExternalInput").ap()
    wq_d = nc.dram_tensor("wq", [128, NFC * NOC * 128], F32, kind="ExternalInput").ap()
    wk_d = nc.dram_tensor("wk", [128, NFC * NOC * 128], F32, kind="ExternalInput").ap()
    wv_d = nc.dram_tensor("wv", [128, NFC * NOC * 128], F32, kind="ExternalInput").ap()
    wo_d = nc.dram_tensor("wo", [128, NOC * HID], F32, kind="ExternalInput").ap()
    wt_d = nc.dram_tensor("wt", [HID * 16, HID], F32, kind="ExternalInput").ap()
    rmbd_d = nc.dram_tensor("rmbd", [32, 128, 128], F32, kind="ExternalInput").ap()
    upbd_d = nc.dram_tensor("upbd", [32, 128, 512], F32, kind="ExternalInput").ap()
    sel_d = nc.dram_tensor("sel", [128, NOC * 8], F32, kind="ExternalInput").ap()
    selt_d = nc.dram_tensor("selt", [8, NOC * 128], F32, kind="ExternalInput").ap()
    avg8_d = nc.dram_tensor("avg8", [8, 8], F32, kind="ExternalInput").ap()
    one88_d = nc.dram_tensor("one88", [8, 8], F32, kind="ExternalInput").ap()
    lnsc_d = nc.dram_tensor("lnsc", [128, HID], F32, kind="ExternalInput").ap()
    lnbi_d = nc.dram_tensor("lnbi", [128, HID], F32, kind="ExternalInput").ap()
    bo_d = nc.dram_tensor("bo", [128, HID], F32, kind="ExternalInput").ap()
    bq_d = nc.dram_tensor("bq", [128, NOC], F32, kind="ExternalInput").ap()
    bk_d = nc.dram_tensor("bk", [128, NOC], F32, kind="ExternalInput").ap()
    bv_d = nc.dram_tensor("bv", [128, NOC], F32, kind="ExternalInput").ap()
    alpha_d = nc.dram_tensor("alphac", [128, NOC], F32, kind="ExternalInput").ap()
    augp_d = nc.dram_tensor("augpc", [128, NOC], F32, kind="ExternalInput").ap()
    pmask_d = nc.dram_tensor("pmask", [128, 1], F32, kind="ExternalInput").ap()
    cvec_d = nc.dram_tensor("cvec", [128, 6], F32, kind="ExternalInput").ap()
    y_d = nc.dram_tensor("y", [TH, HID], F32, kind="ExternalOutput").ap()

    ws_d = nc.dram_tensor("ws", [128, NSLAB, WSF], BF16, kind="Internal").ap()
    dbg_specs = [("xTb", [128, NFC * TT], BF16), ("qf", [128, NOC * TT], F32),
                 ("kf", [128, NOC * TT], F32), ("vf", [128, NOC * TT], F32),
                 ("attnb", [8, TT], F32), ("qku", [128, NOC * TT], F32),
                 ("innb", [128, NOC * TT], BF16), ("yb", [128, SUB * HID], F32),
                 ("x2b", [128, SUB * HID], F32), ("ln1b", [128, SUB * HID], BF16),
                 ("ffnyb", [128, SUB * HID], F32), ("carry", [128, NOC], F32),
                 ("gam", [128, NOC], F32),
                 ("l1sum", [128, 2], F32), ("l1sq", [128, 2], F32),
                 ("l1negm", [128, 2], F32), ("l1var", [128, 2], F32),
                 ("l1is", [128, 2], F32), ("lisum", [128, 2], F32),
                 ("liis", [128, 2], F32), ("l4is", [128, 2], F32),
                 ("gt", [128, NOC * TT], F32), ("stf", [128, NOC * TT], F32)]
    dbg_d = {}
    if debug:
        for nm, shape, dt in dbg_specs:
            dbg_d[nm] = nc.dram_tensor("dbg_" + nm, shape, dt, kind="ExternalOutput").ap()

    x_own_t = x_own.rearrange("(t s p) f -> t p s f", p=128, s=SUB)
    x_pre_t = x_pre.rearrange("(t s p) f -> t p s f", p=128, s=SUB)
    y_t = y_d.rearrange("(t s p) f -> t p s f", p=128, s=SUB)

    ctx = ExitStack()
    sb = lambda n, shape, dt=F32: ctx.enter_context(nc.sbuf_tensor(n, shape, dt))
    psb = lambda n, shape, dt=F32: ctx.enter_context(nc.psum_tensor(n, shape, dt))

    xb = [sb(f"xb{j}", [128, SUB * HID]) for j in range(2)]
    ln1b = sb("ln1b", [128, SUB * HID], BF16)
    ln3b = sb("ln3b", [128, SUB * HID], BF16)
    xTa = sb("xTa", [128, NFC * TT], BF16)
    xTf = [sb(f"xTf{j}", [128, NFC * TT], BF16) for j in range(2)]
    qf = sb("qf", [128, NOC * TT])
    kf = sb("kf", [128, NOC * TT])
    vf = sb("vf", [128, NOC * TT])
    stf = sb("stf", [128, NOC * TT])
    qku = sb("qku", [128, NOC * TT])
    gt = sb("gt", [128, NOC * TT])
    tanb = sb("tanb", [128, NOC * TT])
    innb = sb("innb", [128, NOC * TT], BF16)
    yb = sb("yb", [128, SUB * HID])
    x2b = [sb(f"x2b{j}", [128, SUB * HID]) for j in range(2)]
    ffnyb = sb("ffnyb", [128, SUB * HID])
    outb = [sb(f"outb{j}", [128, SUB * HID]) for j in range(2)]
    sqb = sb("sqb", [128, HID])
    carry = sb("carry", [128, NOC])
    wqb = sb("wqb", [128, NFC * NOC * 128], BF16)
    wkb = sb("wkb", [128, NFC * NOC * 128], BF16)
    wvb = sb("wvb", [128, NFC * NOC * 128], BF16)
    wob = sb("wob", [128, NOC * HID], BF16)
    wsb = [sb(f"wsb{j}", [128, 4 * WSF], BF16) for j in range(3)]
    hb = [sb(f"hb{j}", [128, 512], BF16) for j in range(2)]
    selb = sb("selb", [128, NOC * 8])
    seltb = sb("seltb", [8, NOC * 128])
    avg8b = sb("avg8b", [8, 8])
    one88b = sb("one88b", [8, 8])
    lnscb = sb("lnscb", [128, HID])
    lnbib = sb("lnbib", [128, HID])
    bob = sb("bob", [128, HID])
    bqc = sb("bqc", [128, NOC])
    bkc = sb("bkc", [128, NOC])
    bvc = sb("bvc", [128, NOC])
    alphac = sb("alphacS", [128, NOC])
    augph = sb("augph", [128, NOC])
    gam = sb("gam", [128, NOC])
    omg = sb("omg", [128, NOC])
    pmaskb = sb("pmaskb", [128, 1])
    cvecb = sb("cvecb", [128, 6])
    cds = [sb(f"cd{i}", [128, 1]) for i in range(5)]
    bqd = [sb(f"bqd{i}", [128, 1]) for i in range(NOC)]
    bkd = [sb(f"bkd{i}", [128, 1]) for i in range(NOC)]
    augd = [sb(f"augd{i}", [128, 1]) for i in range(NOC)]
    lrow = sb("lrow", [8, TT])
    einb = sb("einb", [8, TT])
    erow = sb("erow", [8, TT])
    attnb = sb("attnb", [8, TT])
    lnz8 = sb("lnz8", [8, TT])
    rz8 = sb("rz8", [8, TT])
    stats = {k: {r: sb(f"st_{k}_{r}", [128, 2]) for r in ("sum", "sq", "negm", "var", "is", "msq", "lnv")}
             for k in ("l1", "li", "l2", "l3", "l4")}
    ppin = [sb(f"ppin{j}", [128, 512]) for j in range(2)]
    ppout = [sb(f"ppout{j}", [128, 512], BF16) for j in range(2)]

    ps_a = psb("ps_a", [128, 1024])
    ps_b = psb("ps_b", [128, 1024])
    ps_q = psb("ps_q", [128, 1024])
    ps_z = psb("ps_z", [128, 1024])

    sem_names = ["LX0", "LX1", "LC", "CR0", "CR1", "CU0", "CU1", "CE",
                 "PI0", "PI1", "PO0", "PO1", "OU0", "OU1",
                 "LW0", "LW1", "LW2", "TR", "T", "V", "S"]
    sems = {n: ctx.enter_context(nc.semaphore(n)) for n in sem_names}

    P = Prog()

    with ctx:
        consts = [
            (wqb[:], wq_d, "wqb"), (wkb[:], wk_d, "wkb"), (wvb[:], wv_d, "wvb"),
            (wob[:], wo_d, "wob"), (selb[:], sel_d, "selb"), (seltb[:], selt_d, "seltb"),
            (avg8b[:], avg8_d, "avg8b"), (one88b[:], one88_d, "one88b"),
            (lnscb[:], lnsc_d, "lnscb"),
            (lnbib[:], lnbi_d, "lnbib"), (bob[:], bo_d, "bob"),
            (bqc[:], bq_d, "bqc"), (bkc[:], bk_d, "bkc"), (bvc[:], bv_d, "bvc"),
            (alphac[:], alpha_d, "alphac"), (augph[:], augp_d, "augph"),
            (pmaskb[:], pmask_d, "pmaskb"),
            (cvecb[:], cvec_d, "cvecb"),
        ]
        for j, (dst, src, res) in enumerate(consts):
            if j % 4 == 0 and j > 0:
                w = 16 * j
                if w > P.lastw["gpsimd"].get("LC", 0):
                    P.lastw["gpsimd"]["LC"] = w
                    P.stream["gpsimd"].append(([("LC", w)], None, None))

            def f_ld(g, dst=dst, src=src):
                return g.dma_start(dst, src)
            P.stage("gpsimd", f_ld, writes=(res,), inc=("LC", 16))
        lc_total = ("LC", P.cnt["LC"])
        for _, _, res in consts:
            P.writer[res] = lc_total

        def xload(ti, src_t, slot):
            def f(g, ti=ti, src_t=src_t, slot=slot):
                dst = xb[slot].ap().rearrange("p (s f) -> p s f", s=SUB)
                return g.dma_start(dst, src_t[ti])
            return P.stage("gpsimd", f, writes=(f"xb{slot}",), inc=(f"LX{slot}", 16))

        if n_pre > 0:
            xload(0, x_pre_t, 0)
        if n_pre > 1:
            xload(1, x_pre_t, 1)
        if n_pre == 0 and n_own > 0:
            xload(0, x_own_t, 0)
        if n_pre == 0 and n_own > 1:
            xload(1, x_own_t, 1)

        for i in range(5):
            def f_cd(v, i=i):
                return v.tensor_copy(cds[i][:], cvecb[:, i:i + 1])
            P.stage("vector", f_cd, reads=("cvecb",), writes=(f"cd{i}",))
        for oc in range(NOC):
            def f_bq(v, oc=oc):
                return v.tensor_copy(bqd[oc][:], bqc[:, oc:oc + 1])
            P.stage("vector", f_bq, reads=("bqc",), writes=(f"bqd{oc}",))
            def f_bk(v, oc=oc):
                return v.tensor_copy(bkd[oc][:], bkc[:, oc:oc + 1])
            P.stage("vector", f_bk, reads=("bkc",), writes=(f"bkd{oc}",))
            def f_ag(v, oc=oc):
                return v.tensor_scalar(augd[oc][:], augph[:, oc:oc + 1], 0.5, None, ALU.mult)
            P.stage("vector", f_ag, reads=("augph",), writes=(f"augd{oc}",))

        P.stage("scalar", lambda s: s.activation(gam[:], alphac[:], ACT.Tanh,
                                                 scale=cds[4][:]),
                reads=("alphac", "cd4"), writes=("gam",))
        P.stage("vector", lambda v: v.tensor_scalar(gam[:], gam[:], 0.5, 0.5, ALU.mult, ALU.add),
                reads=("gam",), writes=("gam",))
        P.stage("vector", lambda v: v.tensor_scalar(omg[:], gam[:], -1.0, 1.0, ALU.mult, ALU.add),
                reads=("gam",), writes=("omg",))
        P.stage("vector", lambda v: v.memset(gt[:], 0.0), writes=("gt",))
        for oc in range(NOC):
            def f_gt(v, oc=oc):
                return v.tensor_scalar(gt[:, oc * TT:(oc + 1) * TT], gt[:, oc * TT:(oc + 1) * TT],
                                       gam[:, oc:oc + 1], None, ALU.add)
            P.stage("vector", f_gt, reads=("gam", "gt"), writes=("gt",))
        P.stage("vector", lambda v: v.memset(carry[:], 0.0), writes=("carry",))

        rmbb = ffnyb[:, 0:128]
        upbb = ffnyb[:, 128:640]
        cevb = innb[:, 0:512]
        for j in range(32 if do_cbuild else 0):
            def f_lr(sy, j=j):
                return sy.dma_start(rmbb, rmbd_d[j])
            P.stage("sync", f_lr, writes=("rmbb0", "ffnyb"), inc=("CR0", 16))

            def f_lu(sy, j=j):
                return sy.dma_start(upbb, upbd_d[j])
            P.stage("sync", f_lu, writes=("upbb0", "ffnyb"), inc=("CU0", 16))

            def f_mm(t):
                return t.matmul(ps_a[:, 0:512], rmbb, upbb, start=True, stop=True)
            P.stage("tensor", f_mm, reads=("rmbb0", "upbb0", "ffnyb"), writes=("ps_a",))

            P.stage("vector", lambda v: v.tensor_copy(cevb, ps_a[:, 0:512]),
                    reads=("ps_a",), writes=("cevb", "innb"))

            def f_st(sy, j=j):
                dst = ws_d[:, 4 * j:4 * (j + 1), 1024:1152]
                src2 = innb.ap()[:, 0:512].rearrange("p (k c) -> p k c", k=4)
                return sy.dma_start(dst, src2)
            P.stage("sync", f_st, reads=("cevb", "innb"), inc=("CE", 16))

        pp_i = [0]

        def prepass_step():
            c = pp_i[0]
            if c >= 2 * NSLAB:
                return
            pp_i[0] += 1
            slot = c % 2
            row, half = c // 2, c % 2

            def f_li(sy, row=row, half=half, slot=slot):
                return sy.dma_start(ppin[slot][:],
                                    wt_d[row * 128:(row + 1) * 128, half * 512:(half + 1) * 512])
            P.stage("sync", f_li, writes=(f"ppin{slot}",), inc=(f"PI{slot}", 16))

            def f_cast(sc, slot=slot):
                return sc.copy(ppout[slot][:], ppin[slot][:])
            P.stage("scalar", f_cast, reads=(f"ppin{slot}",), writes=(f"ppout{slot}",))

            def f_so(sy, row=row, half=half, slot=slot):
                return sy.dma_start(ws_d[:, row, half * 512:(half + 1) * 512], ppout[slot][:])
            P.stage("sync", f_so, reads=(f"ppout{slot}",), inc=(f"PO{slot}", 16))

        def ln_tiny(st, key, eps):
            def f1(v):
                return v.tensor_scalar(st["negm"][:], st["sum"][:], -1.0 / HID, None, ALU.mult)
            P.stage("vector", f1, reads=(key,), writes=(key,))
            def f2(v):
                return v.tensor_scalar(st["var"][:], st["sq"][:], 1.0 / HID, None, ALU.mult)
            P.stage("vector", f2, reads=(key,), writes=(key,))
            def f3(v):
                return v.tensor_tensor(st["msq"][:], st["negm"][:], st["negm"][:], ALU.mult)
            P.stage("vector", f3, reads=(key,), writes=(key,))
            def f4(v):
                return v.tensor_tensor(st["var"][:], st["var"][:], st["msq"][:], ALU.subtract)
            P.stage("vector", f4, reads=(key,), writes=(key,))
            epsc, epsr = (cds[0], "cd0") if eps == EPS1 else (cds[1], "cd1")

            def f5(sc, epsc=epsc):
                return sc.activation(st["lnv"][:], st["var"][:], ACT.Ln, bias=epsc[:])
            P.stage("scalar", f5, reads=(key, epsr), writes=(key,))

            def f6(sc):
                return sc.activation(st["is"][:], st["lnv"][:], ACT.Exp, scale=cds[2][:])
            P.stage("scalar", f6, reads=(key, "cd2"), writes=(key,))

        def emit_trans(srcbuf, srcres, dstbuf, dstres):
            for s in range(SUB):
                for fc in range(NFC):
                    eng = "scalar" if (s * NFC + fc) % 2 == 0 else "sync"

                    def f_tr(e, s=s, fc=fc, srcbuf=srcbuf, dstbuf=dstbuf):
                        return e.dma_start(
                            dstbuf[:, fc * TT + s * 128: fc * TT + (s + 1) * 128],
                            srcbuf[:, s * HID + fc * 128: s * HID + (fc + 1) * 128],
                            transpose=True)
                    P.stage(eng, f_tr, reads=(srcres,), writes=(dstres,),
                            inc=("TR", 16))

        def prefetch_for(gidx):
            nxt = gidx + 2
            if nxt < NT + NT:
                is_own2 = nxt >= NT
                ok = (nxt - NT < n_own) if is_own2 else (nxt < n_pre)
                if ok:
                    xload(nxt - NT if is_own2 else nxt,
                          x_own_t if is_own2 else x_pre_t, nxt % 2)

        PSQ = ("ps_q", "ps_q_l", "ps_q_r")

        def front_gen(ti, own):
            slot = ti % 2
            gidx = (NT + ti) if own else ti
            st1 = stats["l1"]
            xr = f"xb{slot}"
            xs = xb[slot]

            for s in range(SUB):
                def f_sq(v, s=s):
                    return v.scalar_tensor_tensor(
                        sqb[:], xs[:, s * HID:(s + 1) * HID], 0.0,
                        xs[:, s * HID:(s + 1) * HID],
                        ALU.bypass, ALU.mult, accum_out=st1["sq"][:, s:s + 1])
                P.stage("vector", f_sq, reads=(xr,), writes=("sqb", "st_l1"))

            def f_rs(v):
                return v.reduce_sum(st1["sum"][:], xs.ap().rearrange("p (s f) -> p s f", s=SUB),
                                    axis=mybir.AxisListType.X)
            P.stage("vector", f_rs, reads=(xr,), writes=("st_l1",))
            yield
            ln_tiny(st1, "st_l1", EPS1)
            yield
            for s in range(SUB):
                def f_ap(v, s=s):
                    return v.tensor_scalar(
                        ln1b[:, s * HID:(s + 1) * HID], xs[:, s * HID:(s + 1) * HID],
                        st1["negm"][:, s:s + 1], st1["is"][:, s:s + 1], ALU.add, ALU.mult)
                P.stage("vector", f_ap, reads=(xr, "st_l1"), writes=("ln1b",))
            emit_trans(ln1b, "ln1b", xTa, "xTa")
            if not own:
                prefetch_for(gidx)
            yield

            def mk_qkv(wbuf, wres):
                def f(t, wbuf=wbuf):
                    last = None
                    for oc in range(NOC):
                        for fc in range(NFC):
                            last = t.matmul(
                                ps_q[:, oc * TT:(oc + 1) * TT],
                                wbuf[:, (fc * NOC + oc) * 128:(fc * NOC + oc + 1) * 128],
                                xTa[:, fc * TT:(fc + 1) * TT],
                                start=(fc == 0), stop=(fc == NFC - 1))
                    return last
                return f

            P.stage("tensor", mk_qkv(wqb, "wqb"), reads=("xTa", "wqb"), writes=PSQ)
            yield

            def f_qe(sc):
                last = None
                for oc in range(NOC):
                    last = sc.activation(qf[:, oc * TT:(oc + 1) * TT],
                                         ps_q[:, oc * TT:(oc + 1) * TT],
                                         ACT.Identity, bias=bqd[oc][:])
                return last
            P.stage("scalar", f_qe, reads=PSQ + ("bqd0", "bqd1", "bqd2", "bqd3"),
                    writes=("qf",))
            P.stage("tensor", mk_qkv(wkb, "wkb"), reads=("xTa", "wkb"), writes=PSQ)
            yield

            def f_ke(sc):
                last = None
                for oc in range(NOC):
                    last = sc.activation(kf[:, oc * TT:(oc + 1) * TT],
                                         ps_q[:, oc * TT:(oc + 1) * TT],
                                         ACT.Identity, bias=bkd[oc][:])
                return last
            P.stage("scalar", f_ke, reads=PSQ + ("bkd0", "bkd1", "bkd2", "bkd3"),
                    writes=("kf",))
            P.stage("vector", lambda v: v.tensor_tensor(qku[:], qf[:], kf[:], ALU.mult),
                    reads=("qf", "kf"), writes=("qku",))
            P.stage("tensor", mk_qkv(wvb, "wvb"), reads=("xTa", "wvb"), writes=PSQ)
            yield

            def f_ve(v):
                last = None
                for oc in range(NOC):
                    last = v.tensor_scalar(vf[:, oc * TT:(oc + 1) * TT],
                                           ps_q[:, oc * TT:(oc + 1) * TT],
                                           bvc[:, oc:oc + 1], None, ALU.add)
                return last
            P.stage("vector", f_ve, reads=PSQ + ("bvc",), writes=("vf",))

            def f_log(t):
                last = None
                for oc in range(NOC):
                    last = t.matmul(ps_q[0:8, 0:TT], selb[:, oc * 8:(oc + 1) * 8],
                                    qku[:, oc * TT:(oc + 1) * TT],
                                    start=(oc == 0), stop=(oc == NOC - 1))
                return last
            P.stage("tensor", f_log, reads=("qku", "selb", "vf"), writes=("ps_q_l",))
            yield
            P.stage("vector", lambda v: v.tensor_copy(lrow[:], ps_q[0:8, 0:TT]),
                    reads=("ps_q_l",), writes=("lrow",))
            P.stage("tensor", lambda t: t.matmul(ps_q[0:8, 256:512], avg8b[:], lrow[:],
                                                 start=True, stop=True),
                    reads=("lrow", "avg8b"), writes=("ps_q_r",))
            P.stage("vector", lambda v: v.tensor_tensor(einb[:], lrow[:],
                                                        ps_q[0:8, 256:512], ALU.add),
                    reads=("lrow", "ps_q_r"), writes=("einb",))
            P.stage("scalar", lambda sc: sc.activation(erow[:], einb[:], ACT.Exp),
                    reads=("einb",), writes=("erow",))
            yield
            P.stage("tensor", lambda t: t.matmul(ps_q[0:8, 256:512], one88b[:], erow[:],
                                                 start=True, stop=True),
                    reads=("erow", "one88b"), writes=("ps_q_r",))
            P.stage("scalar", lambda sc: sc.activation(lnz8[:], ps_q[0:8, 256:512], ACT.Ln),
                    reads=("ps_q_r",), writes=("lnz8",))
            P.stage("scalar", lambda sc: sc.activation(rz8[:], lnz8[:], ACT.Exp,
                                                       scale=cds[3][0:8, :]),
                    reads=("lnz8", "cd3"), writes=("rz8",))
            P.stage("vector", lambda v: v.tensor_tensor(attnb[:], erow[:], rz8[:], ALU.mult),
                    reads=("erow", "rz8"), writes=("attnb",))
            yield

            def f_bc(t):
                last = None
                for oc in range(NOC):
                    last = t.matmul(ps_q[:, oc * TT:(oc + 1) * TT],
                                    seltb[:, oc * 128:(oc + 1) * 128], attnb[:],
                                    start=True, stop=True)
                return last
            P.stage("tensor", f_bc, reads=("attnb", "seltb"), writes=PSQ)
            yield

            def f_u(v):
                last = None
                for oc in range(NOC):
                    last = v.scalar_tensor_tensor(
                        qku[:, oc * TT:(oc + 1) * TT], ps_q[:, oc * TT:(oc + 1) * TT],
                        omg[:, oc:oc + 1], vf[:, oc * TT:(oc + 1) * TT], ALU.mult, ALU.mult)
                return last
            P.stage("vector", f_u, reads=PSQ + ("omg", "vf", "qku"), writes=("qku",))

            def f_scan(v):
                last = None
                for oc in range(NOC):
                    last = v.tensor_tensor_scan(
                        stf[:, oc * TT:(oc + 1) * TT], gt[:, oc * TT:(oc + 1) * TT],
                        qku[:, oc * TT:(oc + 1) * TT], carry[:, oc:oc + 1], ALU.mult, ALU.add)
                return last
            P.stage("vector", f_scan, reads=("gt", "qku", "carry"), writes=("stf",))

            def f_cs(v):
                src = stf.ap().rearrange("p (o t) -> p o t", o=NOC)[:, :, TT - 1:TT]
                dst = carry.ap().rearrange("p (o c) -> p o c", c=1)
                return v.tensor_copy(dst, src)
            P.stage("vector", f_cs, reads=("stf",), writes=("carry",))
            if not own:
                return
            yield
            for _ in chain_tail(ti):
                yield

        def chain_tail(ti):
            slot = ti % 2
            xr = f"xb{slot}"
            xs = xb[slot]

            def f_aug(sc):
                last = None
                for oc in range(NOC):
                    last = sc.activation(tanb[:, oc * TT:(oc + 1) * TT],
                                         qf[:, oc * TT:(oc + 1) * TT],
                                         ACT.Tanh, scale=augd[oc][:])
                return last
            P.stage("scalar", f_aug, reads=("qf", "augd0", "augd1", "augd2", "augd3"),
                    writes=("tanb",))
            P.stage("vector", lambda v: v.tensor_tensor(stf[:], qf[:], stf[:], ALU.mult),
                    reads=("qf", "stf"), writes=("stf",))
            P.stage("vector", lambda v: v.scalar_tensor_tensor(stf[:], tanb[:], 0.5, stf[:],
                                                               ALU.mult, ALU.add),
                    reads=("tanb", "stf"), writes=("stf",))
            P.stage("vector", lambda v: v.tensor_scalar(innb[:], stf[:], 0.5, None, ALU.add),
                    reads=("stf",), writes=("innb",))
            yield

            sti = stats["li"]
            for tc in range(SUB):
                def f_op(t, tc=tc):
                    last = None
                    for half in range(2):
                        for oc in range(NOC):
                            last = t.matmul(
                                ps_q[:, half * 512:(half + 1) * 512],
                                innb[:, oc * TT + tc * 128: oc * TT + (tc + 1) * 128],
                                wob[:, oc * HID + half * 512: oc * HID + (half + 1) * 512],
                                start=(oc == 0), stop=(oc == NOC - 1))
                    return last
                P.stage("tensor", f_op, reads=("innb", "wob"), writes=PSQ)

                def f_ye(v, tc=tc):
                    return v.scalar_tensor_tensor(
                        yb[:, tc * HID:(tc + 1) * HID], ps_q[:, 0:HID], 0.0, bob[:],
                        ALU.bypass, ALU.add, accum_out=sti["sum"][:, tc:tc + 1])
                P.stage("vector", f_ye, reads=PSQ + ("bob",), writes=("yb", "st_li"))
                yield

            for tc in range(SUB):
                def f_sqi(v, tc=tc):
                    return v.scalar_tensor_tensor(
                        sqb[:], yb[:, tc * HID:(tc + 1) * HID], 0.0,
                        yb[:, tc * HID:(tc + 1) * HID],
                        ALU.bypass, ALU.mult, accum_out=sti["sq"][:, tc:tc + 1])
                P.stage("vector", f_sqi, reads=("yb",), writes=("sqb", "st_li"))
            ln_tiny(sti, "st_li", EPSI)
            yield
            st2 = stats["l2"]
            for tc in range(SUB):
                def f_a1(v, tc=tc):
                    return v.tensor_scalar(yb[:, tc * HID:(tc + 1) * HID],
                                           yb[:, tc * HID:(tc + 1) * HID],
                                           sti["negm"][:, tc:tc + 1], None, ALU.add)
                P.stage("vector", f_a1, reads=("yb", "st_li"), writes=("yb",))

                def f_a2(v, tc=tc):
                    return v.scalar_tensor_tensor(yb[:, tc * HID:(tc + 1) * HID],
                                                  yb[:, tc * HID:(tc + 1) * HID],
                                                  sti["is"][:, tc:tc + 1], lnscb[:],
                                                  ALU.mult, ALU.mult)
                P.stage("vector", f_a2, reads=("yb", "st_li", "lnscb"), writes=("yb",))

                def f_a3(v, tc=tc):
                    return v.scalar_tensor_tensor(yb[:, tc * HID:(tc + 1) * HID],
                                                  yb[:, tc * HID:(tc + 1) * HID], 0.0, lnbib[:],
                                                  ALU.bypass, ALU.add,
                                                  accum_out=st2["sum"][:, tc:tc + 1])
                P.stage("vector", f_a3, reads=("yb", "lnbib"), writes=("yb", "st_l2"))

            for tc in range(SUB):
                def f_sq2(v, tc=tc):
                    return v.scalar_tensor_tensor(
                        sqb[:], yb[:, tc * HID:(tc + 1) * HID], 0.0,
                        yb[:, tc * HID:(tc + 1) * HID],
                        ALU.bypass, ALU.mult, accum_out=st2["sq"][:, tc:tc + 1])
                P.stage("vector", f_sq2, reads=("yb",), writes=("sqb", "st_l2"))
            ln_tiny(st2, "st_l2", EPS1)
            yield
            st3 = stats["l3"]
            for tc in range(SUB):
                def f_b1(v, tc=tc):
                    return v.tensor_scalar(yb[:, tc * HID:(tc + 1) * HID],
                                           yb[:, tc * HID:(tc + 1) * HID],
                                           st2["negm"][:, tc:tc + 1], st2["is"][:, tc:tc + 1],
                                           ALU.add, ALU.mult)
                P.stage("vector", f_b1, reads=("yb", "st_l2"), writes=("yb",))

                def f_b2(v, tc=tc):
                    return v.scalar_tensor_tensor(
                        x2b[slot][:, tc * HID:(tc + 1) * HID], yb[:, tc * HID:(tc + 1) * HID],
                        0.0, xs[:, tc * HID:(tc + 1) * HID], ALU.bypass, ALU.add,
                        accum_out=st3["sum"][:, tc:tc + 1])
                P.stage("vector", f_b2, reads=("yb", xr), writes=(f"x2b{slot}", "st_l3"))
            prefetch_for(NT + ti)
            yield

            for tc in range(SUB):
                def f_sq3(v, tc=tc):
                    return v.scalar_tensor_tensor(
                        sqb[:], x2b[slot][:, tc * HID:(tc + 1) * HID], 0.0,
                        x2b[slot][:, tc * HID:(tc + 1) * HID], ALU.bypass, ALU.mult,
                        accum_out=st3["sq"][:, tc:tc + 1])
                P.stage("vector", f_sq3, reads=(f"x2b{slot}",), writes=("sqb", "st_l3"))
            ln_tiny(st3, "st_l3", EPS1)
            yield
            for tc in range(SUB):
                def f_ap3(v, tc=tc):
                    return v.tensor_scalar(
                        ln3b[:, tc * HID:(tc + 1) * HID], x2b[slot][:, tc * HID:(tc + 1) * HID],
                        st3["negm"][:, tc:tc + 1], st3["is"][:, tc:tc + 1], ALU.add, ALU.mult)
                P.stage("vector", f_ap3, reads=(f"x2b{slot}", "st_l3"), writes=("ln3b",))
            emit_trans(ln3b, "ln3b", xTf[slot], f"xTf{slot}")

        def ffn_slab_gen(ti):
            slot = ti % 2
            first_cp = True

            def emit_wl(q):
                wslot = q % 3

                def f_wl(sy, q=q, wslot=wslot):
                    src = ws_d[:, 4 * q:4 * (q + 1), :]
                    dst = wsb[wslot].ap().rearrange("p (k c) -> p k c", k=4)
                    return sy.dma_start(dst, src)
                extra = ()
                if ti == 0 and q < 3:
                    extra = [("PO0", P.cnt.get("PO0", 0)), ("PO1", P.cnt.get("PO1", 0)),
                             ("CE", P.cnt.get("CE", 0))]
                P.stage("sync", f_wl, writes=(f"wsb{wslot}",), inc=(f"LW{wslot}", 16),
                        extra=extra)

            def emit_z(sp):
                s0, zsl = 2 * sp, sp % 2

                def f_z(t, s0=s0, zsl=zsl):
                    last = None
                    for ds in range(2):
                        s = s0 + ds
                        wslot_ = (s // 4) % 3
                        rc = (s // 4) % 8
                        coff = (s % 4) * WSF + 1024
                        last = t.matmul(
                            ps_z[:, zsl * 512 + ds * 256: zsl * 512 + (ds + 1) * 256],
                            wsb[wslot_][:, coff:coff + 128],
                            xTf[slot][:, rc * TT:(rc + 1) * TT], start=True, stop=True)
                    return last
                P.stage("tensor", f_z, reads=(f"xTf{slot}", f"wsb{(s0 // 4) % 3}"),
                        writes=(f"ps_z{zsl}",))

                def f_gel(sc, sp=sp, zsl=zsl):
                    return sc.activation(hb[sp % 2][:], ps_z[:, zsl * 512:(zsl + 1) * 512],
                                         ACT.Gelu)
                P.stage("scalar", f_gel, reads=(f"ps_z{zsl}",), writes=(f"hb{sp % 2}",))

            emit_wl(0)
            emit_wl(1)
            emit_z(0)
            for sp in range(NSLAB // 2):
                s0 = 2 * sp
                if s0 % 4 == 0 and s0 // 4 + 2 < NSLAB // 4:
                    emit_wl(s0 // 4 + 2)
                if sp + 1 < NSLAB // 2:
                    emit_z(sp + 1)

                def f_cp(t, s0=s0, sp=sp):
                    last = None
                    for ds in range(2):
                        s = s0 + ds
                        wslot_ = (s // 4) % 3
                        woff = (s % 4) * WSF
                        for tc in range(SUB):
                            ps = ps_a if tc == 0 else ps_b
                            for half in range(2):
                                last = t.matmul(
                                    ps[:, half * 512:(half + 1) * 512],
                                    hb[sp % 2][:, ds * 256 + tc * 128: ds * 256 + (tc + 1) * 128],
                                    wsb[wslot_][:, woff + half * 512: woff + (half + 1) * 512],
                                    start=(s == 0), stop=(s == NSLAB - 1))
                    return last
                P.stage("tensor", f_cp, reads=(f"hb{sp % 2}", f"wsb{(s0 // 4) % 3}"),
                        writes=("ps_a", "ps_b") if first_cp else ())
                first_cp = False
                if sp % 4 == 3 and sp < NSLAB // 2 - 1:
                    yield

            st4 = stats["l4"]
            P.writer["ps_a"] = ("T", P.cnt["T"])
            P.writer["ps_b"] = ("T", P.cnt["T"])
            P.readers["ps_a"] = []
            P.readers["ps_b"] = []
            for tc in range(SUB):
                ps, psr = (ps_a, "ps_a") if tc == 0 else (ps_b, "ps_b")

                def f_ce(v, tc=tc, ps=ps):
                    return v.tensor_scalar(ffnyb[:, tc * HID:(tc + 1) * HID], ps[:, 0:HID],
                                           0.0, 0.0, ALU.add, ALU.add,
                                           accum_out=st4["sum"][:, tc:tc + 1])
                P.stage("vector", f_ce, reads=(psr,), writes=("ffnyb", "st_l4"))

        def ffn_tail(ti):
            slot = ti % 2
            st4 = stats["l4"]

            for tc in range(SUB):
                def f_sq4(v, tc=tc):
                    return v.scalar_tensor_tensor(
                        sqb[:], ffnyb[:, tc * HID:(tc + 1) * HID], 0.0,
                        ffnyb[:, tc * HID:(tc + 1) * HID], ALU.bypass, ALU.mult,
                        accum_out=st4["sq"][:, tc:tc + 1])
                P.stage("vector", f_sq4, reads=("ffnyb",), writes=("sqb", "st_l4"))
            ln_tiny(st4, "st_l4", EPS1)
            ob, obr = outb[ti % 2], f"outb{ti % 2}"
            for tc in range(SUB):
                def f_a4(v, tc=tc):
                    return v.tensor_scalar(ffnyb[:, tc * HID:(tc + 1) * HID],
                                           ffnyb[:, tc * HID:(tc + 1) * HID],
                                           st4["negm"][:, tc:tc + 1], st4["is"][:, tc:tc + 1],
                                           ALU.add, ALU.mult)
                P.stage("vector", f_a4, reads=("ffnyb", "st_l4"), writes=("ffnyb",))

            P.stage("vector", lambda v: v.tensor_tensor(ob[:], ffnyb[:], x2b[slot][:], ALU.add),
                    reads=("ffnyb", f"x2b{slot}"), writes=(obr,))

            def f_out(sy, ti=ti, ob=ob):
                src = ob.ap().rearrange("p (s f) -> p s f", s=SUB)
                return sy.dma_start(y_t[ti], src)
            P.stage("sync", f_out, reads=(obr,), inc=(f"OU{ti % 2}", 16))

        def exhaust(g):
            for _ in g:
                pass

        for ti in range(n_pre):
            for _ in front_gen(ti, False):
                if do_prepass:
                    prepass_step()
                    prepass_step()
        if do_prepass:
            while pp_i[0] < 2 * NSLAB:
                prepass_step()

        P.stage("vector", lambda v: v.tensor_scalar(carry[:], carry[:], pmaskb[:, 0:1], None,
                                                    ALU.mult),
                reads=("carry", "pmaskb"), writes=("carry",))

        bufmap = {}

        dbg_res = {"x2b": "x2b1", "l1sum": "st_l1", "l1sq": "st_l1", "l1negm": "st_l1", "l1var": "st_l1",
                   "l1is": "st_l1", "lisum": "st_li", "liis": "st_li", "l4is": "st_l4"}

        def emit_dbg():
            for nm, shape, dt in dbg_specs:
                buf = bufmap[nm]
                def f_d(sy, nm=nm, buf=buf):
                    return sy.dma_start(dbg_d[nm], buf[:])
                P.stage("sync", f_d, reads=(dbg_res.get(nm, nm),), inc=("OU0", 16))

        for ti in range(n_own):
            fg = front_gen(ti, True)
            if ti == 0:
                exhaust(fg)
            else:
                zg = ffn_slab_gen(ti - 1)
                done_f = done_z = False
                while not (done_f and done_z):
                    if not done_f:
                        done_f = next(fg, "END") == "END"
                    if not done_f:
                        done_f = next(fg, "END") == "END"
                    if not done_z:
                        done_z = next(zg, "END") == "END"
                ffn_tail(ti - 1)
                if debug and ti == 1:
                    bufmap.update(dict(xTb=xTa, qf=qf, kf=kf, vf=vf, attnb=attnb, qku=qku,
                                       innb=innb, yb=yb, x2b=x2b[1], ln1b=ln1b, ffnyb=ffnyb,
                                       carry=carry, gam=gam,
                                       l1sum=stats["l1"]["sum"], l1sq=stats["l1"]["sq"],
                                       l1negm=stats["l1"]["negm"], l1var=stats["l1"]["var"],
                                       l1is=stats["l1"]["is"], lisum=stats["li"]["sum"],
                                       liis=stats["li"]["is"], l4is=stats["l4"]["is"],
                                       gt=gt, stf=stf))
                    emit_dbg()
        if n_own > 0:
            exhaust(ffn_slab_gen(n_own - 1))
            ffn_tail(n_own - 1)

        P.barrier_wait("sync", [("OU0", P.cnt.get("OU0", 0)), ("OU1", P.cnt.get("OU1", 0))])

        with nc.Block() as block:
            def mk(eng_name):
                drain_eng = eng_name in ("vector", "scalar")

                def body(eng):
                    for waits, fn, inc in P.stream[eng_name]:
                        for smn, val in waits:
                            eng.wait_ge(sems[smn], val)
                        if fn is None:
                            continue
                        last = fn(eng)
                        smn, amt = inc
                        if drain_eng and smn in ("V", "S"):
                            last = eng.drain()
                        last.then_inc(sems[smn], amt)
                return body

            block.tensor(mk("tensor"))
            block.vector(mk("vector"))
            block.scalar(mk("scalar"))
            block.gpsimd(mk("gpsimd"))
            block.sync(mk("sync"))

    return nc


_NC_CACHE = None


def kernel(**inputs):
    global _NC_CACHE
    x = np.asarray(inputs["L0_attn_x"], dtype=np.float32)
    Wq = np.asarray(inputs["L0_attn_Wq"], dtype=np.float32)
    bq = np.asarray(inputs["L0_attn_bq"], dtype=np.float32)
    Wk = np.asarray(inputs["L0_attn_Wk"], dtype=np.float32)
    bk = np.asarray(inputs["L0_attn_bk"], dtype=np.float32)
    Wv = np.asarray(inputs["L0_attn_Wv"], dtype=np.float32)
    bv = np.asarray(inputs["L0_attn_bv"], dtype=np.float32)
    Wo = np.asarray(inputs["L0_attn_Wo"], dtype=np.float32)
    bo = np.asarray(inputs["L0_attn_bo"], dtype=np.float32)
    alpha = np.asarray(inputs["L0_attn_alpha"], dtype=np.float32)
    augp = np.asarray(inputs["L0_attn_aug_param"], dtype=np.float32)
    lns = np.asarray(inputs["L0_attn_ln_scale"], dtype=np.float32)
    lnb = np.asarray(inputs["L0_attn_ln_bias"], dtype=np.float32)
    rm = np.asarray(inputs["L0_ffn_rail_maker"], dtype=np.float32)
    up = np.asarray(inputs["L0_ffn_rail_up_proj"], dtype=np.float32)
    wc = np.asarray(inputs["L0_ffn_w_cproj"], dtype=np.float32)

    wq_h = np.ascontiguousarray(Wq.reshape(NFC, 128, NOC, 128).transpose(1, 0, 2, 3)
                                .reshape(128, NFC * NOC * 128))
    wk_h = np.ascontiguousarray(Wk.reshape(NFC, 128, NOC, 128).transpose(1, 0, 2, 3)
                                .reshape(128, NFC * NOC * 128))
    wv_h = np.ascontiguousarray(Wv.reshape(NFC, 128, NOC, 128).transpose(1, 0, 2, 3)
                                .reshape(128, NFC * NOC * 128))
    wo_h = np.ascontiguousarray(Wo.reshape(NOC, 128, HID).transpose(1, 0, 2)
                                .reshape(128, NOC * HID))
    wt_h = np.ascontiguousarray(wc.T)

    rmbd = np.zeros((32, 128, 128), np.float32)
    upbd = np.zeros((32, 128, 512), np.float32)
    for h in range(4):
        for rc in range(8):
            j = h * 8 + rc
            for r8 in range(8):
                r = rc * 8 + r8
                rmbd[j, r8 * 16:(r8 + 1) * 16, r8 * 16:(r8 + 1) * 16] = rm[h, r].T
                upbd[j, r8 * 16:(r8 + 1) * 16, r8 * 64:(r8 + 1) * 64] = up[h, r].T

    heads_of = (np.arange(INNER) // HD)
    sel_h = np.zeros((NOC, 128, 8), np.float32)
    selt_h = np.zeros((NOC, 8, 128), np.float32)
    for oc in range(NOC):
        for p in range(128):
            hh = heads_of[oc * 128 + p]
            sel_h[oc, p, hh] = float(HD) ** -0.5
            selt_h[oc, hh, p] = 1.0
    sel_h = np.ascontiguousarray(sel_h.transpose(1, 0, 2).reshape(128, NOC * 8))
    selt_h = np.ascontiguousarray(selt_h.transpose(1, 0, 2).reshape(8, NOC * 128))

    col = lambda a: np.ascontiguousarray(a.reshape(NOC, 128).T)
    shared = {
        "wq": wq_h, "wk": wk_h, "wv": wv_h, "wo": wo_h, "wt": wt_h,
        "rmbd": rmbd, "upbd": upbd, "sel": sel_h, "selt": selt_h,
        "avg8": np.full((8, 8), -0.125, np.float32), "one88": np.ones((8, 8), np.float32),
        "lnsc": np.ascontiguousarray(np.broadcast_to(lns, (128, HID))),
        "lnbi": np.ascontiguousarray(np.broadcast_to(lnb, (128, HID))),
        "bo": np.ascontiguousarray(np.broadcast_to(bo, (128, HID))),
        "bq": col(bq), "bk": col(bk), "bv": col(bv),
        "cvec": np.ascontiguousarray(np.broadcast_to(
            np.array([EPS1, EPSI, -0.5, -1.0, 0.5, 1.0], np.float32), (128, 6))),
        "alphac": col(alpha.reshape(INNER)), "augpc": col(augp.reshape(INNER)),
    }

    zeros_pre = np.zeros((TH, HID), np.float32)
    ones_m = np.ones((128, 1), np.float32)
    zeros_m = np.zeros((128, 1), np.float32)

    in_maps = []
    for c in range(8):
        b, hf = c // 2, c % 2
        m = dict(shared)
        m["x_own"] = np.ascontiguousarray(x[b, hf * TH:(hf + 1) * TH])
        m["x_pre"] = np.ascontiguousarray(x[b, 0:TH]) if hf else zeros_pre
        m["pmask"] = ones_m if hf else zeros_m
        in_maps.append(m)

    if _NC_CACHE is None:
        _NC_CACHE = build_nc()
    res = run_bass_kernel_spmd(_NC_CACHE, in_maps, core_ids=list(range(8)))

    out = np.empty((NB, SEQ, HID), np.float32)
    for c in range(8):
        b, hf = c // 2, c % 2
        out[b, hf * TH:(hf + 1) * TH] = res.results[c]["y"]
    return out


if __name__ == "__main__":
    print("building...")
    nc = build_nc()
    n_inst = 0
    try:
        n_inst = len(list(nc.all_instructions()))
    except Exception as e:
        print("count failed:", e)
    print("built, instructions:", n_inst)


# version 10
# speedup vs baseline: 1.0069x; 1.0069x over previous
import numpy as np
import concourse.bass as bass
import concourse.mybir as mybir
from concourse.bass_utils import run_bass_kernel_spmd
from contextlib import ExitStack

F32 = mybir.dt.float32
BF16 = mybir.dt.bfloat16
ALU = mybir.AluOpType
ACT = mybir.ActivationFunctionType

HID = 1024
INNER = 512
HD = 64
NB = 4
SEQ = 8192
TH = 4096
TT = 256
SUB = TT // 128
NT = TH // TT
NOC = INNER // 128
NFC = HID // 128
NSLAB = 128
WSF = 1152
EPS1 = 1e-5
EPSI = 1e-6

ENGS = ("tensor", "vector", "scalar", "gpsimd", "sync")
TILE_PHASE = 99


class Prog:

    def __init__(self):
        self.stream = {e: [] for e in ENGS}
        self.cnt = {}
        self.lastw = {e: {} for e in ENGS}
        self.writer = {}
        self.readers = {}
        self.eng_sem = {"tensor": "T", "vector": "V", "scalar": "S"}

    def _want(self, eng, waits, sem, val):
        if val <= 0:
            return
        if sem == self.eng_sem.get(eng):
            return
        if val > self.lastw[eng].get(sem, 0):
            self.lastw[eng][sem] = val
            waits.append((sem, val))

    def stage(self, eng, fn, reads=(), writes=(), inc=None, extra=()):
        waits = []
        for sem, val in extra:
            self._want(eng, waits, sem, val)
        for r in reads:
            w = self.writer.get(r)
            if w:
                self._want(eng, waits, *w)
        for r in writes:
            w = self.writer.get(r)
            if w:
                self._want(eng, waits, *w)
            for rd in self.readers.get(r, ()):
                self._want(eng, waits, *rd)
        if inc is None:
            sem, amt = self.eng_sem[eng], 1
        else:
            sem, amt = inc
        self.cnt[sem] = self.cnt.get(sem, 0) + amt
        mark = (sem, self.cnt[sem])
        for r in reads:
            self.readers.setdefault(r, []).append(mark)
        for r in writes:
            self.writer[r] = mark
            self.readers[r] = []
        self.stream[eng].append((waits, fn, (sem, amt)))
        return mark

    def barrier_wait(self, eng, marks):
        waits = []
        for sem, val in marks:
            self._want(eng, waits, sem, val)
        self.stream[eng].append((waits, None, None))


def build_nc(n_pre=NT, n_own=NT, do_cbuild=True, do_prepass=True, debug=False):
    nc = bass.Bass("TRN2", target_bir_lowering=False, debug=False,
                   detect_race_conditions=False)

    x_own = nc.dram_tensor("x_own", [TH, HID], F32, kind="ExternalInput").ap()
    x_pre = nc.dram_tensor("x_pre", [TH, HID], F32, kind="ExternalInput").ap()
    wq_d = nc.dram_tensor("wq", [128, NFC * NOC * 128], F32, kind="ExternalInput").ap()
    wk_d = nc.dram_tensor("wk", [128, NFC * NOC * 128], F32, kind="ExternalInput").ap()
    wv_d = nc.dram_tensor("wv", [128, NFC * NOC * 128], F32, kind="ExternalInput").ap()
    wo_d = nc.dram_tensor("wo", [128, NOC * HID], F32, kind="ExternalInput").ap()
    wt_d = nc.dram_tensor("wt", [HID * 16, HID], F32, kind="ExternalInput").ap()
    rmbd_d = nc.dram_tensor("rmbd", [32, 128, 128], F32, kind="ExternalInput").ap()
    upbd_d = nc.dram_tensor("upbd", [32, 128, 512], F32, kind="ExternalInput").ap()
    sel_d = nc.dram_tensor("sel", [128, NOC * 8], F32, kind="ExternalInput").ap()
    selt_d = nc.dram_tensor("selt", [8, NOC * 128], F32, kind="ExternalInput").ap()
    avg8_d = nc.dram_tensor("avg8", [8, 8], F32, kind="ExternalInput").ap()
    one88_d = nc.dram_tensor("one88", [8, 8], F32, kind="ExternalInput").ap()
    lnsc_d = nc.dram_tensor("lnsc", [128, HID], F32, kind="ExternalInput").ap()
    lnbi_d = nc.dram_tensor("lnbi", [128, HID], F32, kind="ExternalInput").ap()
    bo_d = nc.dram_tensor("bo", [128, HID], F32, kind="ExternalInput").ap()
    bq_d = nc.dram_tensor("bq", [128, NOC], F32, kind="ExternalInput").ap()
    bk_d = nc.dram_tensor("bk", [128, NOC], F32, kind="ExternalInput").ap()
    bv_d = nc.dram_tensor("bv", [128, NOC], F32, kind="ExternalInput").ap()
    alpha_d = nc.dram_tensor("alphac", [128, NOC], F32, kind="ExternalInput").ap()
    augp_d = nc.dram_tensor("augpc", [128, NOC], F32, kind="ExternalInput").ap()
    pmask_d = nc.dram_tensor("pmask", [128, 1], F32, kind="ExternalInput").ap()
    cvec_d = nc.dram_tensor("cvec", [128, 6], F32, kind="ExternalInput").ap()
    y_d = nc.dram_tensor("y", [TH, HID], F32, kind="ExternalOutput").ap()

    ws_d = nc.dram_tensor("ws", [128, NSLAB, WSF], BF16, kind="Internal").ap()
    dbg_specs = [("xTb", [128, NFC * TT], BF16), ("qf", [128, NOC * TT], F32),
                 ("kf", [128, NOC * TT], F32), ("vf", [128, NOC * TT], F32),
                 ("attnb", [8, TT], F32), ("qku", [128, NOC * TT], F32),
                 ("innb", [128, NOC * TT], BF16), ("yb", [128, SUB * HID], F32),
                 ("x2b", [128, SUB * HID], F32), ("ln1b", [128, SUB * HID], BF16),
                 ("ffnyb", [128, SUB * HID], F32), ("carry", [128, NOC], F32),
                 ("gam", [128, NOC], F32),
                 ("l1sum", [128, 2], F32), ("l1sq", [128, 2], F32),
                 ("l1negm", [128, 2], F32), ("l1var", [128, 2], F32),
                 ("l1is", [128, 2], F32), ("lisum", [128, 2], F32),
                 ("liis", [128, 2], F32), ("l4is", [128, 2], F32),
                 ("gt", [128, NOC * TT], F32), ("stf", [128, NOC * TT], F32)]
    dbg_d = {}
    if debug:
        for nm, shape, dt in dbg_specs:
            dbg_d[nm] = nc.dram_tensor("dbg_" + nm, shape, dt, kind="ExternalOutput").ap()

    x_own_t = x_own.rearrange("(t s p) f -> t p s f", p=128, s=SUB)
    x_pre_t = x_pre.rearrange("(t s p) f -> t p s f", p=128, s=SUB)
    y_t = y_d.rearrange("(t s p) f -> t p s f", p=128, s=SUB)

    ctx = ExitStack()
    sb = lambda n, shape, dt=F32: ctx.enter_context(nc.sbuf_tensor(n, shape, dt))
    psb = lambda n, shape, dt=F32: ctx.enter_context(nc.psum_tensor(n, shape, dt))

    xb = [sb(f"xb{j}", [128, SUB * HID]) for j in range(2)]
    ln1b = sb("ln1b", [128, SUB * HID], BF16)
    ln3b = sb("ln3b", [128, SUB * HID], BF16)
    xTa = sb("xTa", [128, NFC * TT], BF16)
    xTf = [sb(f"xTf{j}", [128, NFC * TT], BF16) for j in range(2)]
    qf = sb("qf", [128, NOC * TT])
    kf = sb("kf", [128, NOC * TT])
    vf = sb("vf", [128, NOC * TT])
    stf = sb("stf", [128, NOC * TT])
    qku = sb("qku", [128, NOC * TT])
    gt = sb("gt", [128, NOC * TT])
    tanb = sb("tanb", [128, NOC * TT])
    innb = sb("innb", [128, NOC * TT], BF16)
    yb = sb("yb", [128, SUB * HID])
    x2b = [sb(f"x2b{j}", [128, SUB * HID]) for j in range(2)]
    ffnyb = sb("ffnyb", [128, SUB * HID])
    outb = [sb(f"outb{j}", [128, SUB * HID]) for j in range(2)]
    sqb = sb("sqb", [128, HID])
    carry = sb("carry", [128, NOC])
    wqb = sb("wqb", [128, NFC * NOC * 128], BF16)
    wkb = sb("wkb", [128, NFC * NOC * 128], BF16)
    wvb = sb("wvb", [128, NFC * NOC * 128], BF16)
    wob = sb("wob", [128, NOC * HID], BF16)
    wsb = [sb(f"wsb{j}", [128, 4 * WSF], BF16) for j in range(3)]
    hb = [sb(f"hb{j}", [128, 512], BF16) for j in range(2)]
    selb = sb("selb", [128, NOC * 8])
    seltb = sb("seltb", [8, NOC * 128])
    avg8b = sb("avg8b", [8, 8])
    one88b = sb("one88b", [8, 8])
    lnscb = sb("lnscb", [128, HID])
    lnbib = sb("lnbib", [128, HID])
    bob = sb("bob", [128, HID])
    bqc = sb("bqc", [128, NOC])
    bkc = sb("bkc", [128, NOC])
    bvc = sb("bvc", [128, NOC])
    alphac = sb("alphacS", [128, NOC])
    augph = sb("augph", [128, NOC])
    gam = sb("gam", [128, NOC])
    omg = sb("omg", [128, NOC])
    pmaskb = sb("pmaskb", [128, 1])
    cvecb = sb("cvecb", [128, 6])
    cds = [sb(f"cd{i}", [128, 1]) for i in range(5)]
    bqd = [sb(f"bqd{i}", [128, 1]) for i in range(NOC)]
    bkd = [sb(f"bkd{i}", [128, 1]) for i in range(NOC)]
    augd = [sb(f"augd{i}", [128, 1]) for i in range(NOC)]
    lrow = sb("lrow", [8, TT])
    einb = sb("einb", [8, TT])
    erow = sb("erow", [8, TT])
    attnb = sb("attnb", [8, TT])
    lnz8 = sb("lnz8", [8, TT])
    rz8 = sb("rz8", [8, TT])
    stats = {k: {r: sb(f"st_{k}_{r}", [128, 2]) for r in ("sum", "sq", "negm", "var", "is", "msq", "lnv")}
             for k in ("l1", "li", "l2", "l3", "l4")}
    ppin = [sb(f"ppin{j}", [128, 512]) for j in range(2)]
    ppout = [sb(f"ppout{j}", [128, 512], BF16) for j in range(2)]

    ps_a = psb("ps_a", [128, 1024])
    ps_b = psb("ps_b", [128, 1024])
    ps_q = psb("ps_q", [128, 1024])
    ps_z = psb("ps_z", [128, 1024])

    sem_names = ["LX0", "LX1", "LC", "CR0", "CR1", "CU0", "CU1", "CE",
                 "PI0", "PI1", "PO0", "PO1", "OU0", "OU1",
                 "LW0", "LW1", "LW2", "TR", "T", "V", "S"]
    sems = {n: ctx.enter_context(nc.semaphore(n)) for n in sem_names}

    P = Prog()

    with ctx:
        consts = [
            (wqb[:], wq_d, "wqb"), (wkb[:], wk_d, "wkb"), (wvb[:], wv_d, "wvb"),
            (wob[:], wo_d, "wob"), (selb[:], sel_d, "selb"), (seltb[:], selt_d, "seltb"),
            (avg8b[:], avg8_d, "avg8b"), (one88b[:], one88_d, "one88b"),
            (lnscb[:], lnsc_d, "lnscb"),
            (lnbib[:], lnbi_d, "lnbib"), (bob[:], bo_d, "bob"),
            (bqc[:], bq_d, "bqc"), (bkc[:], bk_d, "bkc"), (bvc[:], bv_d, "bvc"),
            (alphac[:], alpha_d, "alphac"), (augph[:], augp_d, "augph"),
            (pmaskb[:], pmask_d, "pmaskb"),
            (cvecb[:], cvec_d, "cvecb"),
        ]
        for j, (dst, src, res) in enumerate(consts):
            if j % 4 == 0 and j > 0:
                w = 16 * j
                if w > P.lastw["gpsimd"].get("LC", 0):
                    P.lastw["gpsimd"]["LC"] = w
                    P.stream["gpsimd"].append(([("LC", w)], None, None))

            def f_ld(g, dst=dst, src=src):
                return g.dma_start(dst, src)
            P.stage("gpsimd", f_ld, writes=(res,), inc=("LC", 16))
        lc_total = ("LC", P.cnt["LC"])
        for _, _, res in consts:
            P.writer[res] = lc_total

        def xload(ti, src_t, slot):
            def f(g, ti=ti, src_t=src_t, slot=slot):
                dst = xb[slot].ap().rearrange("p (s f) -> p s f", s=SUB)
                return g.dma_start(dst, src_t[ti])
            return P.stage("gpsimd", f, writes=(f"xb{slot}",), inc=(f"LX{slot}", 16))

        if n_pre > 0:
            xload(0, x_pre_t, 0)
        if n_pre > 1:
            xload(1, x_pre_t, 1)
        if n_pre == 0 and n_own > 0:
            xload(0, x_own_t, 0)
        if n_pre == 0 and n_own > 1:
            xload(1, x_own_t, 1)

        for i in range(5):
            def f_cd(v, i=i):
                return v.tensor_copy(cds[i][:], cvecb[:, i:i + 1])
            P.stage("vector", f_cd, reads=("cvecb",), writes=(f"cd{i}",))
        for oc in range(NOC):
            def f_bq(v, oc=oc):
                return v.tensor_copy(bqd[oc][:], bqc[:, oc:oc + 1])
            P.stage("vector", f_bq, reads=("bqc",), writes=(f"bqd{oc}",))
            def f_bk(v, oc=oc):
                return v.tensor_copy(bkd[oc][:], bkc[:, oc:oc + 1])
            P.stage("vector", f_bk, reads=("bkc",), writes=(f"bkd{oc}",))
            def f_ag(v, oc=oc):
                return v.tensor_scalar(augd[oc][:], augph[:, oc:oc + 1], 0.5, None, ALU.mult)
            P.stage("vector", f_ag, reads=("augph",), writes=(f"augd{oc}",))

        P.stage("scalar", lambda s: s.activation(gam[:], alphac[:], ACT.Tanh,
                                                 scale=cds[4][:]),
                reads=("alphac", "cd4"), writes=("gam",))
        P.stage("vector", lambda v: v.tensor_scalar(gam[:], gam[:], 0.5, 0.5, ALU.mult, ALU.add),
                reads=("gam",), writes=("gam",))
        P.stage("vector", lambda v: v.tensor_scalar(omg[:], gam[:], -1.0, 1.0, ALU.mult, ALU.add),
                reads=("gam",), writes=("omg",))
        P.stage("vector", lambda v: v.memset(gt[:], 0.0), writes=("gt",))
        for oc in range(NOC):
            def f_gt(v, oc=oc):
                return v.tensor_scalar(gt[:, oc * TT:(oc + 1) * TT], gt[:, oc * TT:(oc + 1) * TT],
                                       gam[:, oc:oc + 1], None, ALU.add)
            P.stage("vector", f_gt, reads=("gam", "gt"), writes=("gt",))
        P.stage("vector", lambda v: v.memset(carry[:], 0.0), writes=("carry",))

        rmbb = ffnyb[:, 0:128]
        upbb = ffnyb[:, 128:640]
        cevb = innb[:, 0:512]
        for j in range(32 if do_cbuild else 0):
            def f_lr(sy, j=j):
                return sy.dma_start(rmbb, rmbd_d[j])
            P.stage("sync", f_lr, writes=("rmbb0", "ffnyb"), inc=("CR0", 16))

            def f_lu(sy, j=j):
                return sy.dma_start(upbb, upbd_d[j])
            P.stage("sync", f_lu, writes=("upbb0", "ffnyb"), inc=("CU0", 16))

            def f_mm(t):
                return t.matmul(ps_a[:, 0:512], rmbb, upbb, start=True, stop=True)
            P.stage("tensor", f_mm, reads=("rmbb0", "upbb0", "ffnyb"), writes=("ps_a",))

            P.stage("vector", lambda v: v.tensor_copy(cevb, ps_a[:, 0:512]),
                    reads=("ps_a",), writes=("cevb", "innb"))

            def f_st(sy, j=j):
                dst = ws_d[:, 4 * j:4 * (j + 1), 1024:1152]
                src2 = innb.ap()[:, 0:512].rearrange("p (k c) -> p k c", k=4)
                return sy.dma_start(dst, src2)
            P.stage("sync", f_st, reads=("cevb", "innb"), inc=("CE", 16))

        pp_i = [0]

        def prepass_step():
            c = pp_i[0]
            if c >= 2 * NSLAB:
                return
            pp_i[0] += 1
            slot = c % 2
            row, half = c // 2, c % 2

            def f_li(sy, row=row, half=half, slot=slot):
                return sy.dma_start(ppin[slot][:],
                                    wt_d[row * 128:(row + 1) * 128, half * 512:(half + 1) * 512])
            P.stage("sync", f_li, writes=(f"ppin{slot}",), inc=(f"PI{slot}", 16))

            def f_cast(sc, slot=slot):
                return sc.copy(ppout[slot][:], ppin[slot][:])
            P.stage("scalar", f_cast, reads=(f"ppin{slot}",), writes=(f"ppout{slot}",))

            def f_so(sy, row=row, half=half, slot=slot):
                return sy.dma_start(ws_d[:, row, half * 512:(half + 1) * 512], ppout[slot][:])
            P.stage("sync", f_so, reads=(f"ppout{slot}",), inc=(f"PO{slot}", 16))

        def ln_tiny(st, key, eps):
            def f1(v):
                return v.tensor_scalar(st["negm"][:], st["sum"][:], -1.0 / HID, None, ALU.mult)
            P.stage("vector", f1, reads=(key,), writes=(key,))
            def f2(v):
                return v.tensor_scalar(st["var"][:], st["sq"][:], 1.0 / HID, None, ALU.mult)
            P.stage("vector", f2, reads=(key,), writes=(key,))
            def f3(v):
                return v.tensor_tensor(st["msq"][:], st["negm"][:], st["negm"][:], ALU.mult)
            P.stage("vector", f3, reads=(key,), writes=(key,))
            def f4(v):
                return v.tensor_tensor(st["var"][:], st["var"][:], st["msq"][:], ALU.subtract)
            P.stage("vector", f4, reads=(key,), writes=(key,))
            epsc, epsr = (cds[0], "cd0") if eps == EPS1 else (cds[1], "cd1")

            def f5(sc, epsc=epsc):
                return sc.activation(st["lnv"][:], st["var"][:], ACT.Ln, bias=epsc[:])
            P.stage("scalar", f5, reads=(key, epsr), writes=(key,))

            def f6(sc):
                return sc.activation(st["is"][:], st["lnv"][:], ACT.Exp, scale=cds[2][:])
            P.stage("scalar", f6, reads=(key, "cd2"), writes=(key,))

        def emit_trans(srcbuf, srcres, dstbuf, dstres):
            for s in range(SUB):
                for fc in range(NFC):
                    eng = "scalar" if (s * NFC + fc) % 2 == 0 else "sync"

                    def f_tr(e, s=s, fc=fc, srcbuf=srcbuf, dstbuf=dstbuf):
                        return e.dma_start(
                            dstbuf[:, fc * TT + s * 128: fc * TT + (s + 1) * 128],
                            srcbuf[:, s * HID + fc * 128: s * HID + (fc + 1) * 128],
                            transpose=True)
                    P.stage(eng, f_tr, reads=(srcres,), writes=(dstres,),
                            inc=("TR", 16))

        def prefetch_for(gidx):
            nxt = gidx + 2
            if nxt < NT + NT:
                is_own2 = nxt >= NT
                ok = (nxt - NT < n_own) if is_own2 else (nxt < n_pre)
                if ok:
                    xload(nxt - NT if is_own2 else nxt,
                          x_own_t if is_own2 else x_pre_t, nxt % 2)

        PSQ = ("ps_q", "ps_q_l", "ps_q_r")

        def front_gen(ti, own):
            slot = ti % 2
            gidx = (NT + ti) if own else ti
            st1 = stats["l1"]
            xr = f"xb{slot}"
            xs = xb[slot]

            for s in range(SUB):
                def f_sq(v, s=s):
                    return v.scalar_tensor_tensor(
                        sqb[:], xs[:, s * HID:(s + 1) * HID], 0.0,
                        xs[:, s * HID:(s + 1) * HID],
                        ALU.bypass, ALU.mult, accum_out=st1["sq"][:, s:s + 1])
                P.stage("vector", f_sq, reads=(xr,), writes=("sqb", "st_l1"))

            def f_rs(v):
                return v.reduce_sum(st1["sum"][:], xs.ap().rearrange("p (s f) -> p s f", s=SUB),
                                    axis=mybir.AxisListType.X)
            P.stage("vector", f_rs, reads=(xr,), writes=("st_l1",))
            yield
            ln_tiny(st1, "st_l1", EPS1)
            yield
            for s in range(SUB):
                def f_ap(v, s=s):
                    return v.tensor_scalar(
                        ln1b[:, s * HID:(s + 1) * HID], xs[:, s * HID:(s + 1) * HID],
                        st1["negm"][:, s:s + 1], st1["is"][:, s:s + 1], ALU.add, ALU.mult)
                P.stage("vector", f_ap, reads=(xr, "st_l1"), writes=("ln1b",))
            emit_trans(ln1b, "ln1b", xTa, "xTa")
            if not own:
                prefetch_for(gidx)
            yield

            def mk_qkv(wbuf, wres):
                def f(t, wbuf=wbuf):
                    last = None
                    for oc in range(NOC):
                        for fc in range(NFC):
                            last = t.matmul(
                                ps_q[:, oc * TT:(oc + 1) * TT],
                                wbuf[:, (fc * NOC + oc) * 128:(fc * NOC + oc + 1) * 128],
                                xTa[:, fc * TT:(fc + 1) * TT],
                                start=(fc == 0), stop=(fc == NFC - 1))
                    return last
                return f

            P.stage("tensor", mk_qkv(wqb, "wqb"), reads=("xTa", "wqb"), writes=PSQ)
            yield

            def f_qe(sc):
                last = None
                for oc in range(NOC):
                    last = sc.activation(qf[:, oc * TT:(oc + 1) * TT],
                                         ps_q[:, oc * TT:(oc + 1) * TT],
                                         ACT.Identity, bias=bqd[oc][:])
                return last
            P.stage("scalar", f_qe, reads=PSQ + ("bqd0", "bqd1", "bqd2", "bqd3"),
                    writes=("qf",))
            P.stage("tensor", mk_qkv(wkb, "wkb"), reads=("xTa", "wkb"), writes=PSQ)
            yield

            def f_ke(sc):
                last = None
                for oc in range(NOC):
                    last = sc.activation(kf[:, oc * TT:(oc + 1) * TT],
                                         ps_q[:, oc * TT:(oc + 1) * TT],
                                         ACT.Identity, bias=bkd[oc][:])
                return last
            P.stage("scalar", f_ke, reads=PSQ + ("bkd0", "bkd1", "bkd2", "bkd3"),
                    writes=("kf",))
            P.stage("vector", lambda v: v.tensor_tensor(qku[:], qf[:], kf[:], ALU.mult),
                    reads=("qf", "kf"), writes=("qku",))
            P.stage("tensor", mk_qkv(wvb, "wvb"), reads=("xTa", "wvb"), writes=PSQ)
            yield

            def f_ve(v):
                last = None
                for oc in range(NOC):
                    last = v.tensor_scalar(vf[:, oc * TT:(oc + 1) * TT],
                                           ps_q[:, oc * TT:(oc + 1) * TT],
                                           bvc[:, oc:oc + 1], None, ALU.add)
                return last
            P.stage("vector", f_ve, reads=PSQ + ("bvc",), writes=("vf",))

            def f_log(t):
                last = None
                for oc in range(NOC):
                    last = t.matmul(ps_q[0:8, 0:TT], selb[:, oc * 8:(oc + 1) * 8],
                                    qku[:, oc * TT:(oc + 1) * TT],
                                    start=(oc == 0), stop=(oc == NOC - 1))
                return last
            P.stage("tensor", f_log, reads=("qku", "selb", "vf"), writes=("ps_q_l",))
            yield
            P.stage("vector", lambda v: v.tensor_copy(lrow[:], ps_q[0:8, 0:TT]),
                    reads=("ps_q_l",), writes=("lrow",))
            P.stage("tensor", lambda t: t.matmul(ps_q[0:8, 256:512], avg8b[:], lrow[:],
                                                 start=True, stop=True),
                    reads=("lrow", "avg8b"), writes=("ps_q_r",))
            P.stage("vector", lambda v: v.tensor_tensor(einb[:], lrow[:],
                                                        ps_q[0:8, 256:512], ALU.add),
                    reads=("lrow", "ps_q_r"), writes=("einb",))
            P.stage("scalar", lambda sc: sc.activation(erow[:], einb[:], ACT.Exp),
                    reads=("einb",), writes=("erow",))
            yield
            P.stage("tensor", lambda t: t.matmul(ps_q[0:8, 256:512], one88b[:], erow[:],
                                                 start=True, stop=True),
                    reads=("erow", "one88b"), writes=("ps_q_r",))
            P.stage("scalar", lambda sc: sc.activation(lnz8[:], ps_q[0:8, 256:512], ACT.Ln),
                    reads=("ps_q_r",), writes=("lnz8",))
            P.stage("scalar", lambda sc: sc.activation(rz8[:], lnz8[:], ACT.Exp,
                                                       scale=cds[3][0:8, :]),
                    reads=("lnz8", "cd3"), writes=("rz8",))
            P.stage("vector", lambda v: v.tensor_tensor(attnb[:], erow[:], rz8[:], ALU.mult),
                    reads=("erow", "rz8"), writes=("attnb",))
            yield

            def f_bc(t):
                last = None
                for oc in range(NOC):
                    last = t.matmul(ps_q[:, oc * TT:(oc + 1) * TT],
                                    seltb[:, oc * 128:(oc + 1) * 128], attnb[:],
                                    start=True, stop=True)
                return last
            P.stage("tensor", f_bc, reads=("attnb", "seltb"), writes=PSQ)
            yield

            def f_u(v):
                last = None
                for oc in range(NOC):
                    last = v.scalar_tensor_tensor(
                        qku[:, oc * TT:(oc + 1) * TT], ps_q[:, oc * TT:(oc + 1) * TT],
                        omg[:, oc:oc + 1], vf[:, oc * TT:(oc + 1) * TT], ALU.mult, ALU.mult)
                return last
            P.stage("vector", f_u, reads=PSQ + ("omg", "vf", "qku"), writes=("qku",))

            def f_scan(v):
                last = None
                for oc in range(NOC):
                    last = v.tensor_tensor_scan(
                        stf[:, oc * TT:(oc + 1) * TT], gt[:, oc * TT:(oc + 1) * TT],
                        qku[:, oc * TT:(oc + 1) * TT], carry[:, oc:oc + 1], ALU.mult, ALU.add)
                return last
            P.stage("vector", f_scan, reads=("gt", "qku", "carry"), writes=("stf",))

            def f_cs(v):
                src = stf.ap().rearrange("p (o t) -> p o t", o=NOC)[:, :, TT - 1:TT]
                dst = carry.ap().rearrange("p (o c) -> p o c", c=1)
                return v.tensor_copy(dst, src)
            P.stage("vector", f_cs, reads=("stf",), writes=("carry",))
            if not own:
                return
            yield
            for _ in chain_tail(ti):
                yield

        def chain_tail(ti):
            slot = ti % 2
            xr = f"xb{slot}"
            xs = xb[slot]

            def f_aug(sc):
                last = None
                for oc in range(NOC):
                    last = sc.activation(tanb[:, oc * TT:(oc + 1) * TT],
                                         qf[:, oc * TT:(oc + 1) * TT],
                                         ACT.Tanh, scale=augd[oc][:])
                return last
            P.stage("scalar", f_aug, reads=("qf", "augd0", "augd1", "augd2", "augd3"),
                    writes=("tanb",))
            P.stage("vector", lambda v: v.tensor_tensor(stf[:], qf[:], stf[:], ALU.mult),
                    reads=("qf", "stf"), writes=("stf",))
            P.stage("vector", lambda v: v.scalar_tensor_tensor(stf[:], tanb[:], 0.5, stf[:],
                                                               ALU.mult, ALU.add),
                    reads=("tanb", "stf"), writes=("stf",))
            P.stage("vector", lambda v: v.tensor_scalar(innb[:], stf[:], 0.5, None, ALU.add),
                    reads=("stf",), writes=("innb",))
            yield

            sti = stats["li"]
            for tc in range(SUB):
                def f_op(t, tc=tc):
                    last = None
                    for half in range(2):
                        for oc in range(NOC):
                            last = t.matmul(
                                ps_q[:, half * 512:(half + 1) * 512],
                                innb[:, oc * TT + tc * 128: oc * TT + (tc + 1) * 128],
                                wob[:, oc * HID + half * 512: oc * HID + (half + 1) * 512],
                                start=(oc == 0), stop=(oc == NOC - 1))
                    return last
                P.stage("tensor", f_op, reads=("innb", "wob"), writes=PSQ)

                def f_ye(v, tc=tc):
                    return v.scalar_tensor_tensor(
                        yb[:, tc * HID:(tc + 1) * HID], ps_q[:, 0:HID], 0.0, bob[:],
                        ALU.bypass, ALU.add, accum_out=sti["sum"][:, tc:tc + 1])
                P.stage("vector", f_ye, reads=PSQ + ("bob",), writes=("yb", "st_li"))
                yield

            for tc in range(SUB):
                def f_sqi(v, tc=tc):
                    return v.scalar_tensor_tensor(
                        sqb[:], yb[:, tc * HID:(tc + 1) * HID], 0.0,
                        yb[:, tc * HID:(tc + 1) * HID],
                        ALU.bypass, ALU.mult, accum_out=sti["sq"][:, tc:tc + 1])
                P.stage("vector", f_sqi, reads=("yb",), writes=("sqb", "st_li"))
            ln_tiny(sti, "st_li", EPSI)
            yield
            st2 = stats["l2"]
            for tc in range(SUB):
                def f_a1(v, tc=tc):
                    return v.tensor_scalar(yb[:, tc * HID:(tc + 1) * HID],
                                           yb[:, tc * HID:(tc + 1) * HID],
                                           sti["negm"][:, tc:tc + 1], None, ALU.add)
                P.stage("vector", f_a1, reads=("yb", "st_li"), writes=("yb",))

                def f_a2(v, tc=tc):
                    return v.scalar_tensor_tensor(yb[:, tc * HID:(tc + 1) * HID],
                                                  yb[:, tc * HID:(tc + 1) * HID],
                                                  sti["is"][:, tc:tc + 1], lnscb[:],
                                                  ALU.mult, ALU.mult)
                P.stage("vector", f_a2, reads=("yb", "st_li", "lnscb"), writes=("yb",))

                def f_a3(v, tc=tc):
                    return v.scalar_tensor_tensor(yb[:, tc * HID:(tc + 1) * HID],
                                                  yb[:, tc * HID:(tc + 1) * HID], 0.0, lnbib[:],
                                                  ALU.bypass, ALU.add,
                                                  accum_out=st2["sum"][:, tc:tc + 1])
                P.stage("vector", f_a3, reads=("yb", "lnbib"), writes=("yb", "st_l2"))

            for tc in range(SUB):
                def f_sq2(v, tc=tc):
                    return v.scalar_tensor_tensor(
                        sqb[:], yb[:, tc * HID:(tc + 1) * HID], 0.0,
                        yb[:, tc * HID:(tc + 1) * HID],
                        ALU.bypass, ALU.mult, accum_out=st2["sq"][:, tc:tc + 1])
                P.stage("vector", f_sq2, reads=("yb",), writes=("sqb", "st_l2"))
            ln_tiny(st2, "st_l2", EPS1)
            yield
            st3 = stats["l3"]
            for tc in range(SUB):
                def f_b1(v, tc=tc):
                    return v.tensor_scalar(yb[:, tc * HID:(tc + 1) * HID],
                                           yb[:, tc * HID:(tc + 1) * HID],
                                           st2["negm"][:, tc:tc + 1], st2["is"][:, tc:tc + 1],
                                           ALU.add, ALU.mult)
                P.stage("vector", f_b1, reads=("yb", "st_l2"), writes=("yb",))

                def f_b2(v, tc=tc):
                    return v.scalar_tensor_tensor(
                        x2b[slot][:, tc * HID:(tc + 1) * HID], yb[:, tc * HID:(tc + 1) * HID],
                        0.0, xs[:, tc * HID:(tc + 1) * HID], ALU.bypass, ALU.add,
                        accum_out=st3["sum"][:, tc:tc + 1])
                P.stage("vector", f_b2, reads=("yb", xr), writes=(f"x2b{slot}", "st_l3"))
            prefetch_for(NT + ti)
            yield

            for tc in range(SUB):
                def f_sq3(v, tc=tc):
                    return v.scalar_tensor_tensor(
                        sqb[:], x2b[slot][:, tc * HID:(tc + 1) * HID], 0.0,
                        x2b[slot][:, tc * HID:(tc + 1) * HID], ALU.bypass, ALU.mult,
                        accum_out=st3["sq"][:, tc:tc + 1])
                P.stage("vector", f_sq3, reads=(f"x2b{slot}",), writes=("sqb", "st_l3"))
            ln_tiny(st3, "st_l3", EPS1)
            yield
            for tc in range(SUB):
                def f_ap3(v, tc=tc):
                    return v.tensor_scalar(
                        ln3b[:, tc * HID:(tc + 1) * HID], x2b[slot][:, tc * HID:(tc + 1) * HID],
                        st3["negm"][:, tc:tc + 1], st3["is"][:, tc:tc + 1], ALU.add, ALU.mult)
                P.stage("vector", f_ap3, reads=(f"x2b{slot}", "st_l3"), writes=("ln3b",))
            emit_trans(ln3b, "ln3b", xTf[slot], f"xTf{slot}")

        def ffn_slab_gen(ti):
            slot = ti % 2
            first_cp = True

            def emit_wl(q):
                wslot = q % 3

                def f_wl(sy, q=q, wslot=wslot):
                    src = ws_d[:, 4 * q:4 * (q + 1), :]
                    dst = wsb[wslot].ap().rearrange("p (k c) -> p k c", k=4)
                    return sy.dma_start(dst, src)
                extra = ()
                if ti == 0 and q < 3:
                    extra = [("PO0", P.cnt.get("PO0", 0)), ("PO1", P.cnt.get("PO1", 0)),
                             ("CE", P.cnt.get("CE", 0))]
                P.stage("sync", f_wl, writes=(f"wsb{wslot}",), inc=(f"LW{wslot}", 16),
                        extra=extra)

            def emit_z(sp):
                s0, zsl = 2 * sp, sp % 2

                def f_z(t, s0=s0, zsl=zsl):
                    last = None
                    for ds in range(2):
                        s = s0 + ds
                        wslot_ = (s // 4) % 3
                        rc = (s // 4) % 8
                        coff = (s % 4) * WSF + 1024
                        last = t.matmul(
                            ps_z[:, zsl * 512 + ds * 256: zsl * 512 + (ds + 1) * 256],
                            wsb[wslot_][:, coff:coff + 128],
                            xTf[slot][:, rc * TT:(rc + 1) * TT], start=True, stop=True)
                    return last
                P.stage("tensor", f_z, reads=(f"xTf{slot}", f"wsb{(s0 // 4) % 3}"),
                        writes=(f"ps_z{zsl}",))

                def f_gel(sc, sp=sp, zsl=zsl):
                    return sc.activation(hb[sp % 2][:], ps_z[:, zsl * 512:(zsl + 1) * 512],
                                         ACT.Gelu)
                P.stage("scalar", f_gel, reads=(f"ps_z{zsl}",), writes=(f"hb{sp % 2}",))

            emit_wl(0)
            emit_wl(1)
            emit_z(0)
            for sp in range(NSLAB // 2):
                s0 = 2 * sp
                if s0 % 4 == 0 and s0 // 4 + 2 < NSLAB // 4:
                    emit_wl(s0 // 4 + 2)
                if sp + 1 < NSLAB // 2:
                    emit_z(sp + 1)

                def f_cp(t, s0=s0, sp=sp):
                    last = None
                    for ds in range(2):
                        s = s0 + ds
                        wslot_ = (s // 4) % 3
                        woff = (s % 4) * WSF
                        for tc in range(SUB):
                            ps = ps_a if tc == 0 else ps_b
                            for half in range(2):
                                last = t.matmul(
                                    ps[:, half * 512:(half + 1) * 512],
                                    hb[sp % 2][:, ds * 256 + tc * 128: ds * 256 + (tc + 1) * 128],
                                    wsb[wslot_][:, woff + half * 512: woff + (half + 1) * 512],
                                    start=(s == 0), stop=(s == NSLAB - 1))
                    return last
                P.stage("tensor", f_cp, reads=(f"hb{sp % 2}", f"wsb{(s0 // 4) % 3}"),
                        writes=("ps_a", "ps_b") if first_cp else ())
                first_cp = False
                if sp % 4 == 3 and sp < NSLAB // 2 - 1:
                    yield

            st4 = stats["l4"]
            P.writer["ps_a"] = ("T", P.cnt["T"])
            P.writer["ps_b"] = ("T", P.cnt["T"])
            P.readers["ps_a"] = []
            P.readers["ps_b"] = []
            for tc in range(SUB):
                ps, psr = (ps_a, "ps_a") if tc == 0 else (ps_b, "ps_b")

                def f_ce(v, tc=tc, ps=ps):
                    return v.tensor_scalar(ffnyb[:, tc * HID:(tc + 1) * HID], ps[:, 0:HID],
                                           0.0, 0.0, ALU.add, ALU.add,
                                           accum_out=st4["sum"][:, tc:tc + 1])
                P.stage("vector", f_ce, reads=(psr,), writes=("ffnyb", "st_l4"))

        def ffn_tail(ti):
            slot = ti % 2
            st4 = stats["l4"]

            for tc in range(SUB):
                def f_sq4(v, tc=tc):
                    return v.scalar_tensor_tensor(
                        sqb[:], ffnyb[:, tc * HID:(tc + 1) * HID], 0.0,
                        ffnyb[:, tc * HID:(tc + 1) * HID], ALU.bypass, ALU.mult,
                        accum_out=st4["sq"][:, tc:tc + 1])
                P.stage("vector", f_sq4, reads=("ffnyb",), writes=("sqb", "st_l4"))
            ln_tiny(st4, "st_l4", EPS1)
            ob, obr = outb[ti % 2], f"outb{ti % 2}"
            for tc in range(SUB):
                def f_a4(v, tc=tc):
                    return v.tensor_scalar(ffnyb[:, tc * HID:(tc + 1) * HID],
                                           ffnyb[:, tc * HID:(tc + 1) * HID],
                                           st4["negm"][:, tc:tc + 1], st4["is"][:, tc:tc + 1],
                                           ALU.add, ALU.mult)
                P.stage("vector", f_a4, reads=("ffnyb", "st_l4"), writes=("ffnyb",))

            P.stage("vector", lambda v: v.tensor_tensor(ob[:], ffnyb[:], x2b[slot][:], ALU.add),
                    reads=("ffnyb", f"x2b{slot}"), writes=(obr,))

            def f_out(sy, ti=ti, ob=ob):
                src = ob.ap().rearrange("p (s f) -> p s f", s=SUB)
                return sy.dma_start(y_t[ti], src)
            P.stage("sync", f_out, reads=(obr,), inc=(f"OU{ti % 2}", 16))

        def exhaust(g):
            for _ in g:
                pass

        for ti in range(n_pre):
            for _ in front_gen(ti, False):
                if do_prepass:
                    prepass_step()
                    prepass_step()
        if do_prepass:
            while pp_i[0] < 2 * NSLAB:
                prepass_step()

        P.stage("vector", lambda v: v.tensor_scalar(carry[:], carry[:], pmaskb[:, 0:1], None,
                                                    ALU.mult),
                reads=("carry", "pmaskb"), writes=("carry",))

        bufmap = {}

        dbg_res = {"x2b": "x2b1", "l1sum": "st_l1", "l1sq": "st_l1", "l1negm": "st_l1", "l1var": "st_l1",
                   "l1is": "st_l1", "lisum": "st_li", "liis": "st_li", "l4is": "st_l4"}

        def emit_dbg():
            for nm, shape, dt in dbg_specs:
                buf = bufmap[nm]
                def f_d(sy, nm=nm, buf=buf):
                    return sy.dma_start(dbg_d[nm], buf[:])
                P.stage("sync", f_d, reads=(dbg_res.get(nm, nm),), inc=("OU0", 16))

        for ti in range(n_own):
            fg = front_gen(ti, True)
            if ti == 0:
                exhaust(fg)
            else:
                zg = ffn_slab_gen(ti - 1)
                done_f = done_z = False
                ff = fz = 0
                for _ in range(3):
                    if not done_f:
                        done_f = next(fg, "END") == "END"
                        ff += 1
                while not (done_f and done_z):
                    if not done_z:
                        done_z = next(zg, "END") == "END"
                        fz += 1
                    want_f = 3 + (fz * 16 + 11) // 12
                    while ff < want_f and not done_f:
                        done_f = next(fg, "END") == "END"
                        ff += 1
                    if done_z and not done_f:
                        done_f = next(fg, "END") == "END"
                ffn_tail(ti - 1)
                if debug and ti == 1:
                    bufmap.update(dict(xTb=xTa, qf=qf, kf=kf, vf=vf, attnb=attnb, qku=qku,
                                       innb=innb, yb=yb, x2b=x2b[1], ln1b=ln1b, ffnyb=ffnyb,
                                       carry=carry, gam=gam,
                                       l1sum=stats["l1"]["sum"], l1sq=stats["l1"]["sq"],
                                       l1negm=stats["l1"]["negm"], l1var=stats["l1"]["var"],
                                       l1is=stats["l1"]["is"], lisum=stats["li"]["sum"],
                                       liis=stats["li"]["is"], l4is=stats["l4"]["is"],
                                       gt=gt, stf=stf))
                    emit_dbg()
        if n_own > 0:
            exhaust(ffn_slab_gen(n_own - 1))
            ffn_tail(n_own - 1)

        P.barrier_wait("sync", [("OU0", P.cnt.get("OU0", 0)), ("OU1", P.cnt.get("OU1", 0))])

        with nc.Block() as block:
            def mk(eng_name):
                drain_eng = eng_name in ("vector", "scalar")

                def body(eng):
                    for waits, fn, inc in P.stream[eng_name]:
                        for smn, val in waits:
                            eng.wait_ge(sems[smn], val)
                        if fn is None:
                            continue
                        last = fn(eng)
                        smn, amt = inc
                        if drain_eng and smn in ("V", "S"):
                            last = eng.drain()
                        last.then_inc(sems[smn], amt)
                return body

            block.tensor(mk("tensor"))
            block.vector(mk("vector"))
            block.scalar(mk("scalar"))
            block.gpsimd(mk("gpsimd"))
            block.sync(mk("sync"))

    return nc


_NC_CACHE = None


def kernel(**inputs):
    global _NC_CACHE
    x = np.asarray(inputs["L0_attn_x"], dtype=np.float32)
    Wq = np.asarray(inputs["L0_attn_Wq"], dtype=np.float32)
    bq = np.asarray(inputs["L0_attn_bq"], dtype=np.float32)
    Wk = np.asarray(inputs["L0_attn_Wk"], dtype=np.float32)
    bk = np.asarray(inputs["L0_attn_bk"], dtype=np.float32)
    Wv = np.asarray(inputs["L0_attn_Wv"], dtype=np.float32)
    bv = np.asarray(inputs["L0_attn_bv"], dtype=np.float32)
    Wo = np.asarray(inputs["L0_attn_Wo"], dtype=np.float32)
    bo = np.asarray(inputs["L0_attn_bo"], dtype=np.float32)
    alpha = np.asarray(inputs["L0_attn_alpha"], dtype=np.float32)
    augp = np.asarray(inputs["L0_attn_aug_param"], dtype=np.float32)
    lns = np.asarray(inputs["L0_attn_ln_scale"], dtype=np.float32)
    lnb = np.asarray(inputs["L0_attn_ln_bias"], dtype=np.float32)
    rm = np.asarray(inputs["L0_ffn_rail_maker"], dtype=np.float32)
    up = np.asarray(inputs["L0_ffn_rail_up_proj"], dtype=np.float32)
    wc = np.asarray(inputs["L0_ffn_w_cproj"], dtype=np.float32)

    wq_h = np.ascontiguousarray(Wq.reshape(NFC, 128, NOC, 128).transpose(1, 0, 2, 3)
                                .reshape(128, NFC * NOC * 128))
    wk_h = np.ascontiguousarray(Wk.reshape(NFC, 128, NOC, 128).transpose(1, 0, 2, 3)
                                .reshape(128, NFC * NOC * 128))
    wv_h = np.ascontiguousarray(Wv.reshape(NFC, 128, NOC, 128).transpose(1, 0, 2, 3)
                                .reshape(128, NFC * NOC * 128))
    wo_h = np.ascontiguousarray(Wo.reshape(NOC, 128, HID).transpose(1, 0, 2)
                                .reshape(128, NOC * HID))
    wt_h = np.ascontiguousarray(wc.T)

    rmbd = np.zeros((32, 128, 128), np.float32)
    upbd = np.zeros((32, 128, 512), np.float32)
    for h in range(4):
        for rc in range(8):
            j = h * 8 + rc
            for r8 in range(8):
                r = rc * 8 + r8
                rmbd[j, r8 * 16:(r8 + 1) * 16, r8 * 16:(r8 + 1) * 16] = rm[h, r].T
                upbd[j, r8 * 16:(r8 + 1) * 16, r8 * 64:(r8 + 1) * 64] = up[h, r].T

    heads_of = (np.arange(INNER) // HD)
    sel_h = np.zeros((NOC, 128, 8), np.float32)
    selt_h = np.zeros((NOC, 8, 128), np.float32)
    for oc in range(NOC):
        for p in range(128):
            hh = heads_of[oc * 128 + p]
            sel_h[oc, p, hh] = float(HD) ** -0.5
            selt_h[oc, hh, p] = 1.0
    sel_h = np.ascontiguousarray(sel_h.transpose(1, 0, 2).reshape(128, NOC * 8))
    selt_h = np.ascontiguousarray(selt_h.transpose(1, 0, 2).reshape(8, NOC * 128))

    col = lambda a: np.ascontiguousarray(a.reshape(NOC, 128).T)
    shared = {
        "wq": wq_h, "wk": wk_h, "wv": wv_h, "wo": wo_h, "wt": wt_h,
        "rmbd": rmbd, "upbd": upbd, "sel": sel_h, "selt": selt_h,
        "avg8": np.full((8, 8), -0.125, np.float32), "one88": np.ones((8, 8), np.float32),
        "lnsc": np.ascontiguousarray(np.broadcast_to(lns, (128, HID))),
        "lnbi": np.ascontiguousarray(np.broadcast_to(lnb, (128, HID))),
        "bo": np.ascontiguousarray(np.broadcast_to(bo, (128, HID))),
        "bq": col(bq), "bk": col(bk), "bv": col(bv),
        "cvec": np.ascontiguousarray(np.broadcast_to(
            np.array([EPS1, EPSI, -0.5, -1.0, 0.5, 1.0], np.float32), (128, 6))),
        "alphac": col(alpha.reshape(INNER)), "augpc": col(augp.reshape(INNER)),
    }

    zeros_pre = np.zeros((TH, HID), np.float32)
    ones_m = np.ones((128, 1), np.float32)
    zeros_m = np.zeros((128, 1), np.float32)

    in_maps = []
    for c in range(8):
        b, hf = c // 2, c % 2
        m = dict(shared)
        m["x_own"] = np.ascontiguousarray(x[b, hf * TH:(hf + 1) * TH])
        m["x_pre"] = np.ascontiguousarray(x[b, 0:TH]) if hf else zeros_pre
        m["pmask"] = ones_m if hf else zeros_m
        in_maps.append(m)

    if _NC_CACHE is None:
        _NC_CACHE = build_nc()
    res = run_bass_kernel_spmd(_NC_CACHE, in_maps, core_ids=list(range(8)))

    out = np.empty((NB, SEQ, HID), np.float32)
    for c in range(8):
        b, hf = c // 2, c % 2
        out[b, hf * TH:(hf + 1) * TH] = res.results[c]["y"]
    return out


if __name__ == "__main__":
    print("building...")
    nc = build_nc()
    n_inst = 0
    try:
        n_inst = len(list(nc.all_instructions()))
    except Exception as e:
        print("count failed:", e)
    print("built, instructions:", n_inst)


# version 12
# speedup vs baseline: 1.0366x; 1.0295x over previous
import numpy as np
import concourse.bass as bass
import concourse.mybir as mybir
from concourse.bass_utils import run_bass_kernel_spmd
from contextlib import ExitStack

F32 = mybir.dt.float32
BF16 = mybir.dt.bfloat16
ALU = mybir.AluOpType
ACT = mybir.ActivationFunctionType

HID = 1024
INNER = 512
HD = 64
NB = 4
SEQ = 8192
TH = 4096
TT = 256
SUB = TT // 128
NT = TH // TT
NOC = INNER // 128
NFC = HID // 128
NSLAB = 128
WSF = 1152
EPS1 = 1e-5
EPSI = 1e-6

ENGS = ("tensor", "vector", "scalar", "gpsimd", "sync")
TILE_PHASE = 99


class Prog:

    def __init__(self):
        self.stream = {e: [] for e in ENGS}
        self.cnt = {}
        self.lastw = {e: {} for e in ENGS}
        self.writer = {}
        self.readers = {}
        self.eng_sem = {"tensor": "T", "vector": "V", "scalar": "S"}

    def _want(self, eng, waits, sem, val):
        if val <= 0:
            return
        if sem == self.eng_sem.get(eng):
            return
        if val > self.lastw[eng].get(sem, 0):
            self.lastw[eng][sem] = val
            waits.append((sem, val))

    def stage(self, eng, fn, reads=(), writes=(), inc=None, extra=()):
        waits = []
        for sem, val in extra:
            self._want(eng, waits, sem, val)
        for r in reads:
            w = self.writer.get(r)
            if w:
                self._want(eng, waits, *w)
        for r in writes:
            w = self.writer.get(r)
            if w:
                self._want(eng, waits, *w)
            for rd in self.readers.get(r, ()):
                self._want(eng, waits, *rd)
        if inc is None:
            sem, amt = self.eng_sem[eng], 1
        else:
            sem, amt = inc
        self.cnt[sem] = self.cnt.get(sem, 0) + amt
        mark = (sem, self.cnt[sem])
        for r in reads:
            self.readers.setdefault(r, []).append(mark)
        for r in writes:
            self.writer[r] = mark
            self.readers[r] = []
        self.stream[eng].append((waits, fn, (sem, amt)))
        return mark

    def barrier_wait(self, eng, marks):
        waits = []
        for sem, val in marks:
            self._want(eng, waits, sem, val)
        self.stream[eng].append((waits, None, None))


def build_nc(n_pre=NT, n_own=NT, do_cbuild=True, do_prepass=True, debug=False):
    nc = bass.Bass("TRN2", target_bir_lowering=False, debug=False,
                   detect_race_conditions=False)

    x_own = nc.dram_tensor("x_own", [TH, HID], F32, kind="ExternalInput").ap()
    x_pre = nc.dram_tensor("x_pre", [TH, HID], F32, kind="ExternalInput").ap()
    wq_d = nc.dram_tensor("wq", [128, NFC * NOC * 128], F32, kind="ExternalInput").ap()
    wk_d = nc.dram_tensor("wk", [128, NFC * NOC * 128], F32, kind="ExternalInput").ap()
    wv_d = nc.dram_tensor("wv", [128, NFC * NOC * 128], F32, kind="ExternalInput").ap()
    wo_d = nc.dram_tensor("wo", [128, NOC * HID], F32, kind="ExternalInput").ap()
    wt_d = nc.dram_tensor("wt", [HID * 16, HID], F32, kind="ExternalInput").ap()
    rmbd_d = nc.dram_tensor("rmbd", [32, 128, 128], F32, kind="ExternalInput").ap()
    upbd_d = nc.dram_tensor("upbd", [32, 128, 512], F32, kind="ExternalInput").ap()
    sel_d = nc.dram_tensor("sel", [128, NOC * 8], F32, kind="ExternalInput").ap()
    selt_d = nc.dram_tensor("selt", [8, NOC * 128], F32, kind="ExternalInput").ap()
    avg8_d = nc.dram_tensor("avg8", [8, 8], F32, kind="ExternalInput").ap()
    one88_d = nc.dram_tensor("one88", [8, 8], F32, kind="ExternalInput").ap()
    lnsc_d = nc.dram_tensor("lnsc", [128, HID], F32, kind="ExternalInput").ap()
    lnbi_d = nc.dram_tensor("lnbi", [128, HID], F32, kind="ExternalInput").ap()
    bo_d = nc.dram_tensor("bo", [128, HID], F32, kind="ExternalInput").ap()
    bq_d = nc.dram_tensor("bq", [128, NOC], F32, kind="ExternalInput").ap()
    bk_d = nc.dram_tensor("bk", [128, NOC], F32, kind="ExternalInput").ap()
    bv_d = nc.dram_tensor("bv", [128, NOC], F32, kind="ExternalInput").ap()
    alpha_d = nc.dram_tensor("alphac", [128, NOC], F32, kind="ExternalInput").ap()
    augp_d = nc.dram_tensor("augpc", [128, NOC], F32, kind="ExternalInput").ap()
    pmask_d = nc.dram_tensor("pmask", [128, 1], F32, kind="ExternalInput").ap()
    cvec_d = nc.dram_tensor("cvec", [128, 6], F32, kind="ExternalInput").ap()
    y_d = nc.dram_tensor("y", [TH, HID], F32, kind="ExternalOutput").ap()

    ws_d = nc.dram_tensor("ws", [128, NSLAB, WSF], BF16, kind="Internal").ap()
    dbg_specs = [("xTb", [128, NFC * TT], BF16), ("qf", [128, NOC * TT], F32),
                 ("kf", [128, NOC * TT], F32), ("vf", [128, NOC * TT], F32),
                 ("attnb", [8, TT], F32), ("qku", [128, NOC * TT], F32),
                 ("innb", [128, NOC * TT], BF16), ("yb", [128, SUB * HID], F32),
                 ("x2b", [128, SUB * HID], F32), ("ln1b", [128, SUB * HID], BF16),
                 ("ffnyb", [128, SUB * HID], F32), ("carry", [128, NOC], F32),
                 ("gam", [128, NOC], F32),
                 ("l1sum", [128, 2], F32), ("l1sq", [128, 2], F32),
                 ("l1negm", [128, 2], F32), ("l1var", [128, 2], F32),
                 ("l1is", [128, 2], F32), ("lisum", [128, 2], F32),
                 ("liis", [128, 2], F32), ("l4is", [128, 2], F32),
                 ("gt", [128, NOC * TT], F32), ("stf", [128, NOC * TT], F32)]
    dbg_d = {}
    if debug:
        for nm, shape, dt in dbg_specs:
            dbg_d[nm] = nc.dram_tensor("dbg_" + nm, shape, dt, kind="ExternalOutput").ap()

    x_own_t = x_own.rearrange("(t s p) f -> t p s f", p=128, s=SUB)
    x_pre_t = x_pre.rearrange("(t s p) f -> t p s f", p=128, s=SUB)
    y_t = y_d.rearrange("(t s p) f -> t p s f", p=128, s=SUB)

    ctx = ExitStack()
    sb = lambda n, shape, dt=F32: ctx.enter_context(nc.sbuf_tensor(n, shape, dt))
    psb = lambda n, shape, dt=F32: ctx.enter_context(nc.psum_tensor(n, shape, dt))

    xb = [sb(f"xb{j}", [128, SUB * HID]) for j in range(2)]
    ln1b = sb("ln1b", [128, SUB * HID], BF16)
    ln3b = sb("ln3b", [128, SUB * HID], BF16)
    xTa = sb("xTa", [128, NFC * TT], BF16)
    xTf = [sb(f"xTf{j}", [128, NFC * TT], BF16) for j in range(2)]
    qf = sb("qf", [128, NOC * TT])
    kf = sb("kf", [128, NOC * TT])
    vf = sb("vf", [128, NOC * TT])
    stf = sb("stf", [128, NOC * TT])
    qku = sb("qku", [128, NOC * TT])
    gt = sb("gt", [128, NOC * TT])
    tanb = sb("tanb", [128, NOC * TT])
    innb = sb("innb", [128, NOC * TT], BF16)
    yb = sb("yb", [128, SUB * HID])
    x2b = [sb(f"x2b{j}", [128, SUB * HID]) for j in range(2)]
    ffnyb = sb("ffnyb", [128, SUB * HID])
    outb = [sb(f"outb{j}", [128, SUB * HID]) for j in range(2)]
    sqb = sb("sqb", [128, HID])
    carry = sb("carry", [128, NOC])
    wqb = sb("wqb", [128, NFC * NOC * 128], BF16)
    wkb = sb("wkb", [128, NFC * NOC * 128], BF16)
    wvb = sb("wvb", [128, NFC * NOC * 128], BF16)
    wob = sb("wob", [128, NOC * HID], BF16)
    wsb = [sb(f"wsb{j}", [128, 4 * WSF], BF16) for j in range(3)]
    hb = [sb(f"hb{j}", [128, 512], BF16) for j in range(2)]
    selb = sb("selb", [128, NOC * 8])
    seltb = sb("seltb", [8, NOC * 128])
    avg8b = sb("avg8b", [8, 8])
    one88b = sb("one88b", [8, 8])
    lnscb = sb("lnscb", [128, HID])
    lnbib = sb("lnbib", [128, HID])
    bob = sb("bob", [128, HID])
    bqc = sb("bqc", [128, NOC])
    bkc = sb("bkc", [128, NOC])
    bvc = sb("bvc", [128, NOC])
    alphac = sb("alphacS", [128, NOC])
    augph = sb("augph", [128, NOC])
    gam = sb("gam", [128, NOC])
    omg = sb("omg", [128, NOC])
    pmaskb = sb("pmaskb", [128, 1])
    cvecb = sb("cvecb", [128, 6])
    cds = [sb(f"cd{i}", [128, 1]) for i in range(5)]
    bqd = [sb(f"bqd{i}", [128, 1]) for i in range(NOC)]
    bkd = [sb(f"bkd{i}", [128, 1]) for i in range(NOC)]
    augd = [sb(f"augd{i}", [128, 1]) for i in range(NOC)]
    lrow = sb("lrow", [8, TT])
    einb = sb("einb", [8, TT])
    erow = sb("erow", [8, TT])
    attnb = sb("attnb", [8, TT])
    lnz8 = sb("lnz8", [8, TT])
    rz8 = sb("rz8", [8, TT])
    stats = {k: {r: sb(f"st_{k}_{r}", [128, 2]) for r in ("sum", "sq", "negm", "var", "is", "msq", "lnv")}
             for k in ("l1", "li", "l2", "l3", "l4")}
    ppin = [sb(f"ppin{j}", [128, 512]) for j in range(2)]
    ppout = [sb(f"ppout{j}", [128, 512], BF16) for j in range(2)]

    ps_a = psb("ps_a", [128, 1024])
    ps_b = psb("ps_b", [128, 1024])
    ps_q = psb("ps_q", [128, 1024])
    ps_z = psb("ps_z", [128, 1024])

    sem_names = ["LX0", "LX1", "LC", "CR0", "CR1", "CU0", "CU1", "CE",
                 "PI0", "PI1", "PO0", "PO1", "OU0", "OU1",
                 "LW0", "LW1", "LW2", "TR", "T", "V", "S"]
    sems = {n: ctx.enter_context(nc.semaphore(n)) for n in sem_names}

    P = Prog()

    with ctx:
        consts = [
            (wqb[:], wq_d, "wqb"), (wkb[:], wk_d, "wkb"), (wvb[:], wv_d, "wvb"),
            (wob[:], wo_d, "wob"), (selb[:], sel_d, "selb"), (seltb[:], selt_d, "seltb"),
            (avg8b[:], avg8_d, "avg8b"), (one88b[:], one88_d, "one88b"),
            (lnscb[:], lnsc_d, "lnscb"),
            (lnbib[:], lnbi_d, "lnbib"), (bob[:], bo_d, "bob"),
            (bqc[:], bq_d, "bqc"), (bkc[:], bk_d, "bkc"), (bvc[:], bv_d, "bvc"),
            (alphac[:], alpha_d, "alphac"), (augph[:], augp_d, "augph"),
            (pmaskb[:], pmask_d, "pmaskb"),
            (cvecb[:], cvec_d, "cvecb"),
        ]
        for j, (dst, src, res) in enumerate(consts):
            if j % 4 == 0 and j > 0:
                w = 16 * j
                if w > P.lastw["gpsimd"].get("LC", 0):
                    P.lastw["gpsimd"]["LC"] = w
                    P.stream["gpsimd"].append(([("LC", w)], None, None))

            def f_ld(g, dst=dst, src=src):
                return g.dma_start(dst, src)
            P.stage("gpsimd", f_ld, writes=(res,), inc=("LC", 16))
        lc_total = ("LC", P.cnt["LC"])
        for _, _, res in consts:
            P.writer[res] = lc_total

        def xload(ti, src_t, slot):
            def f(g, ti=ti, src_t=src_t, slot=slot):
                dst = xb[slot].ap().rearrange("p (s f) -> p s f", s=SUB)
                return g.dma_start(dst, src_t[ti])
            return P.stage("gpsimd", f, writes=(f"xb{slot}",), inc=(f"LX{slot}", 16))

        if n_pre > 0:
            xload(0, x_pre_t, 0)
        if n_pre > 1:
            xload(1, x_pre_t, 1)
        if n_pre == 0 and n_own > 0:
            xload(0, x_own_t, 0)
        if n_pre == 0 and n_own > 1:
            xload(1, x_own_t, 1)

        for i in range(5):
            def f_cd(v, i=i):
                return v.tensor_copy(cds[i][:], cvecb[:, i:i + 1])
            P.stage("vector", f_cd, reads=("cvecb",), writes=(f"cd{i}",))
        for oc in range(NOC):
            def f_bq(v, oc=oc):
                return v.tensor_copy(bqd[oc][:], bqc[:, oc:oc + 1])
            P.stage("vector", f_bq, reads=("bqc",), writes=(f"bqd{oc}",))
            def f_bk(v, oc=oc):
                return v.tensor_copy(bkd[oc][:], bkc[:, oc:oc + 1])
            P.stage("vector", f_bk, reads=("bkc",), writes=(f"bkd{oc}",))
            def f_ag(v, oc=oc):
                return v.tensor_scalar(augd[oc][:], augph[:, oc:oc + 1], 0.5, None, ALU.mult)
            P.stage("vector", f_ag, reads=("augph",), writes=(f"augd{oc}",))

        P.stage("scalar", lambda s: s.activation(gam[:], alphac[:], ACT.Tanh,
                                                 scale=cds[4][:]),
                reads=("alphac", "cd4"), writes=("gam",))
        P.stage("vector", lambda v: v.tensor_scalar(gam[:], gam[:], 0.5, 0.5, ALU.mult, ALU.add),
                reads=("gam",), writes=("gam",))
        P.stage("vector", lambda v: v.tensor_scalar(omg[:], gam[:], -1.0, 1.0, ALU.mult, ALU.add),
                reads=("gam",), writes=("omg",))
        P.stage("vector", lambda v: v.memset(gt[:], 0.0), writes=("gt",))
        for oc in range(NOC):
            def f_gt(v, oc=oc):
                return v.tensor_scalar(gt[:, oc * TT:(oc + 1) * TT], gt[:, oc * TT:(oc + 1) * TT],
                                       gam[:, oc:oc + 1], None, ALU.add)
            P.stage("vector", f_gt, reads=("gam", "gt"), writes=("gt",))
        P.stage("vector", lambda v: v.memset(carry[:], 0.0), writes=("carry",))

        rmbb = ffnyb[:, 0:128]
        upbb = ffnyb[:, 128:640]
        cevb = innb[:, 0:512]
        for j in range(32 if do_cbuild else 0):
            def f_lr(sy, j=j):
                return sy.dma_start(rmbb, rmbd_d[j])
            P.stage("sync", f_lr, writes=("rmbb0", "ffnyb"), inc=("CR0", 16))

            def f_lu(sy, j=j):
                return sy.dma_start(upbb, upbd_d[j])
            P.stage("sync", f_lu, writes=("upbb0", "ffnyb"), inc=("CU0", 16))

            def f_mm(t):
                return t.matmul(ps_a[:, 0:512], rmbb, upbb, start=True, stop=True)
            P.stage("tensor", f_mm, reads=("rmbb0", "upbb0", "ffnyb"), writes=("ps_a",))

            P.stage("vector", lambda v: v.tensor_copy(cevb, ps_a[:, 0:512]),
                    reads=("ps_a",), writes=("cevb", "innb"))

            def f_st(sy, j=j):
                dst = ws_d[:, 4 * j:4 * (j + 1), 1024:1152]
                src2 = innb.ap()[:, 0:512].rearrange("p (k c) -> p k c", k=4)
                return sy.dma_start(dst, src2)
            P.stage("sync", f_st, reads=("cevb", "innb"), inc=("CE", 16))

        pp_i = [0]

        def prepass_step():
            c = pp_i[0]
            if c >= 2 * NSLAB:
                return
            pp_i[0] += 1
            slot = c % 2
            row, half = c // 2, c % 2

            def f_li(sy, row=row, half=half, slot=slot):
                return sy.dma_start(ppin[slot][:],
                                    wt_d[row * 128:(row + 1) * 128, half * 512:(half + 1) * 512])
            P.stage("sync", f_li, writes=(f"ppin{slot}",), inc=(f"PI{slot}", 16))

            def f_cast(sc, slot=slot):
                return sc.copy(ppout[slot][:], ppin[slot][:])
            P.stage("scalar", f_cast, reads=(f"ppin{slot}",), writes=(f"ppout{slot}",))

            def f_so(sy, row=row, half=half, slot=slot):
                return sy.dma_start(ws_d[:, row, half * 512:(half + 1) * 512], ppout[slot][:])
            P.stage("sync", f_so, reads=(f"ppout{slot}",), inc=(f"PO{slot}", 16))

        def ln_tiny(st, key, eps):
            def fv1(v):
                v.tensor_scalar(st["negm"][:], st["sum"][:], -1.0 / HID, None, ALU.mult)
                return v.tensor_scalar(st["var"][:], st["sq"][:], 1.0 / HID, None, ALU.mult)
            P.stage("vector", fv1, reads=(key,), writes=(key,))
            P.stage("vector", lambda v: v.tensor_tensor(st["msq"][:], st["negm"][:],
                                                        st["negm"][:], ALU.mult),
                    reads=(key,), writes=(key,))
            P.stage("vector", lambda v: v.tensor_tensor(st["var"][:], st["var"][:],
                                                        st["msq"][:], ALU.subtract),
                    reads=(key,), writes=(key,))
            epsc, epsr = (cds[0], "cd0") if eps == EPS1 else (cds[1], "cd1")

            def fs(sc, epsc=epsc):
                sc.activation(st["lnv"][:], st["var"][:], ACT.Ln, bias=epsc[:])
                return sc.activation(st["is"][:], st["lnv"][:], ACT.Exp, scale=cds[2][:])
            P.stage("scalar", fs, reads=(key, epsr, "cd2"), writes=(key,))

        def emit_trans(srcbuf, srcres, dstbuf, dstres):
            for s in range(SUB):
                for fc in range(NFC):
                    eng = "scalar" if (s * NFC + fc) % 2 == 0 else "sync"

                    def f_tr(e, s=s, fc=fc, srcbuf=srcbuf, dstbuf=dstbuf):
                        return e.dma_start(
                            dstbuf[:, fc * TT + s * 128: fc * TT + (s + 1) * 128],
                            srcbuf[:, s * HID + fc * 128: s * HID + (fc + 1) * 128],
                            transpose=True)
                    P.stage(eng, f_tr, reads=(srcres,), writes=(dstres,),
                            inc=("TR", 16))

        def prefetch_for(gidx):
            nxt = gidx + 2
            if nxt < NT + NT:
                is_own2 = nxt >= NT
                ok = (nxt - NT < n_own) if is_own2 else (nxt < n_pre)
                if ok:
                    xload(nxt - NT if is_own2 else nxt,
                          x_own_t if is_own2 else x_pre_t, nxt % 2)

        PSQ = ("ps_q", "ps_q_l", "ps_q_r")

        def front_gen(ti, own):
            slot = ti % 2
            gidx = (NT + ti) if own else ti
            st1 = stats["l1"]
            xr = f"xb{slot}"
            xs = xb[slot]

            def f_st1(v):
                for s in range(SUB):
                    v.scalar_tensor_tensor(
                        sqb[:], xs[:, s * HID:(s + 1) * HID], 0.0,
                        xs[:, s * HID:(s + 1) * HID],
                        ALU.bypass, ALU.mult, accum_out=st1["sq"][:, s:s + 1])
                return v.reduce_sum(st1["sum"][:], xs.ap().rearrange("p (s f) -> p s f", s=SUB),
                                    axis=mybir.AxisListType.X)
            P.stage("vector", f_st1, reads=(xr,), writes=("sqb", "st_l1"))
            yield
            ln_tiny(st1, "st_l1", EPS1)
            yield

            def f_ap(v):
                last = None
                for s in range(SUB):
                    last = v.tensor_scalar(
                        ln1b[:, s * HID:(s + 1) * HID], xs[:, s * HID:(s + 1) * HID],
                        st1["negm"][:, s:s + 1], st1["is"][:, s:s + 1], ALU.add, ALU.mult)
                return last
            P.stage("vector", f_ap, reads=(xr, "st_l1"), writes=("ln1b",))
            emit_trans(ln1b, "ln1b", xTa, "xTa")
            if not own:
                prefetch_for(gidx)
            yield

            def mk_qkv(wbuf, wres):
                def f(t, wbuf=wbuf):
                    last = None
                    for oc in range(NOC):
                        for fc in range(NFC):
                            last = t.matmul(
                                ps_q[:, oc * TT:(oc + 1) * TT],
                                wbuf[:, (fc * NOC + oc) * 128:(fc * NOC + oc + 1) * 128],
                                xTa[:, fc * TT:(fc + 1) * TT],
                                start=(fc == 0), stop=(fc == NFC - 1))
                    return last
                return f

            P.stage("tensor", mk_qkv(wqb, "wqb"), reads=("xTa", "wqb"), writes=PSQ)
            yield

            def f_qe(sc):
                last = None
                for oc in range(NOC):
                    last = sc.activation(qf[:, oc * TT:(oc + 1) * TT],
                                         ps_q[:, oc * TT:(oc + 1) * TT],
                                         ACT.Identity, bias=bqd[oc][:])
                return last
            P.stage("scalar", f_qe, reads=PSQ + ("bqd0", "bqd1", "bqd2", "bqd3"),
                    writes=("qf",))
            P.stage("tensor", mk_qkv(wkb, "wkb"), reads=("xTa", "wkb"), writes=PSQ)
            yield

            def f_ke(sc):
                last = None
                for oc in range(NOC):
                    last = sc.activation(kf[:, oc * TT:(oc + 1) * TT],
                                         ps_q[:, oc * TT:(oc + 1) * TT],
                                         ACT.Identity, bias=bkd[oc][:])
                return last
            P.stage("scalar", f_ke, reads=PSQ + ("bkd0", "bkd1", "bkd2", "bkd3"),
                    writes=("kf",))
            P.stage("vector", lambda v: v.tensor_tensor(qku[:], qf[:], kf[:], ALU.mult),
                    reads=("qf", "kf"), writes=("qku",))
            P.stage("tensor", mk_qkv(wvb, "wvb"), reads=("xTa", "wvb"), writes=PSQ)
            yield

            def f_ve(v):
                last = None
                for oc in range(NOC):
                    last = v.tensor_scalar(vf[:, oc * TT:(oc + 1) * TT],
                                           ps_q[:, oc * TT:(oc + 1) * TT],
                                           bvc[:, oc:oc + 1], None, ALU.add)
                return last
            P.stage("vector", f_ve, reads=PSQ + ("bvc",), writes=("vf",))

            def f_log(t):
                last = None
                for oc in range(NOC):
                    last = t.matmul(ps_q[0:8, 0:TT], selb[:, oc * 8:(oc + 1) * 8],
                                    qku[:, oc * TT:(oc + 1) * TT],
                                    start=(oc == 0), stop=(oc == NOC - 1))
                return last
            P.stage("tensor", f_log, reads=("qku", "selb", "vf"), writes=("ps_q_l",))
            yield
            P.stage("vector", lambda v: v.tensor_copy(lrow[:], ps_q[0:8, 0:TT]),
                    reads=("ps_q_l",), writes=("lrow",))
            P.stage("tensor", lambda t: t.matmul(ps_q[0:8, 256:512], avg8b[:], lrow[:],
                                                 start=True, stop=True),
                    reads=("lrow", "avg8b"), writes=("ps_q_r",))
            P.stage("vector", lambda v: v.tensor_tensor(einb[:], lrow[:],
                                                        ps_q[0:8, 256:512], ALU.add),
                    reads=("lrow", "ps_q_r"), writes=("einb",))
            P.stage("scalar", lambda sc: sc.activation(erow[:], einb[:], ACT.Exp),
                    reads=("einb",), writes=("erow",))
            yield
            P.stage("tensor", lambda t: t.matmul(ps_q[0:8, 256:512], one88b[:], erow[:],
                                                 start=True, stop=True),
                    reads=("erow", "one88b"), writes=("ps_q_r",))
            def f_rz(sc):
                sc.activation(lnz8[:], ps_q[0:8, 256:512], ACT.Ln)
                return sc.activation(rz8[:], lnz8[:], ACT.Exp, scale=cds[3][0:8, :])
            P.stage("scalar", f_rz, reads=("ps_q_r", "cd3"), writes=("lnz8", "rz8"))
            P.stage("vector", lambda v: v.tensor_tensor(attnb[:], erow[:], rz8[:], ALU.mult),
                    reads=("erow", "rz8"), writes=("attnb",))
            yield

            def f_bc(t):
                last = None
                for oc in range(NOC):
                    last = t.matmul(ps_q[:, oc * TT:(oc + 1) * TT],
                                    seltb[:, oc * 128:(oc + 1) * 128], attnb[:],
                                    start=True, stop=True)
                return last
            P.stage("tensor", f_bc, reads=("attnb", "seltb"), writes=PSQ)
            yield

            def f_u(v):
                last = None
                for oc in range(NOC):
                    last = v.scalar_tensor_tensor(
                        qku[:, oc * TT:(oc + 1) * TT], ps_q[:, oc * TT:(oc + 1) * TT],
                        omg[:, oc:oc + 1], vf[:, oc * TT:(oc + 1) * TT], ALU.mult, ALU.mult)
                return last
            P.stage("vector", f_u, reads=PSQ + ("omg", "vf", "qku"), writes=("qku",))

            def f_scan(v):
                last = None
                for oc in range(NOC):
                    last = v.tensor_tensor_scan(
                        stf[:, oc * TT:(oc + 1) * TT], gt[:, oc * TT:(oc + 1) * TT],
                        qku[:, oc * TT:(oc + 1) * TT], carry[:, oc:oc + 1], ALU.mult, ALU.add)
                return last
            P.stage("vector", f_scan, reads=("gt", "qku", "carry"), writes=("stf",))

            def f_cs(v):
                src = stf.ap().rearrange("p (o t) -> p o t", o=NOC)[:, :, TT - 1:TT]
                dst = carry.ap().rearrange("p (o c) -> p o c", c=1)
                return v.tensor_copy(dst, src)
            P.stage("vector", f_cs, reads=("stf",), writes=("carry",))
            if not own:
                return
            yield
            for _ in chain_tail(ti):
                yield

        def chain_tail(ti):
            slot = ti % 2
            xr = f"xb{slot}"
            xs = xb[slot]

            def f_aug(sc):
                last = None
                for oc in range(NOC):
                    last = sc.activation(tanb[:, oc * TT:(oc + 1) * TT],
                                         qf[:, oc * TT:(oc + 1) * TT],
                                         ACT.Tanh, scale=augd[oc][:])
                return last
            P.stage("scalar", f_aug, reads=("qf", "augd0", "augd1", "augd2", "augd3"),
                    writes=("tanb",))
            P.stage("vector", lambda v: v.tensor_tensor(stf[:], qf[:], stf[:], ALU.mult),
                    reads=("qf", "stf"), writes=("stf",))
            P.stage("vector", lambda v: v.scalar_tensor_tensor(stf[:], tanb[:], 0.5, stf[:],
                                                               ALU.mult, ALU.add),
                    reads=("tanb", "stf"), writes=("stf",))
            P.stage("vector", lambda v: v.tensor_scalar(innb[:], stf[:], 0.5, None, ALU.add),
                    reads=("stf",), writes=("innb",))
            yield

            sti = stats["li"]
            for tc in range(SUB):
                def f_op(t, tc=tc):
                    last = None
                    for half in range(2):
                        for oc in range(NOC):
                            last = t.matmul(
                                ps_q[:, half * 512:(half + 1) * 512],
                                innb[:, oc * TT + tc * 128: oc * TT + (tc + 1) * 128],
                                wob[:, oc * HID + half * 512: oc * HID + (half + 1) * 512],
                                start=(oc == 0), stop=(oc == NOC - 1))
                    return last
                P.stage("tensor", f_op, reads=("innb", "wob"), writes=PSQ)

                def f_ye(v, tc=tc):
                    return v.scalar_tensor_tensor(
                        yb[:, tc * HID:(tc + 1) * HID], ps_q[:, 0:HID], 0.0, bob[:],
                        ALU.bypass, ALU.add, accum_out=sti["sum"][:, tc:tc + 1])
                P.stage("vector", f_ye, reads=PSQ + ("bob",), writes=("yb", "st_li"))
                yield

            for tc in range(SUB):
                def f_sqi(v, tc=tc):
                    return v.scalar_tensor_tensor(
                        sqb[:], yb[:, tc * HID:(tc + 1) * HID], 0.0,
                        yb[:, tc * HID:(tc + 1) * HID],
                        ALU.bypass, ALU.mult, accum_out=sti["sq"][:, tc:tc + 1])
                P.stage("vector", f_sqi, reads=("yb",), writes=("sqb", "st_li"))
            ln_tiny(sti, "st_li", EPSI)
            yield
            st2 = stats["l2"]

            def f_lni1(v):
                last = None
                for tc in range(SUB):
                    yy = yb[:, tc * HID:(tc + 1) * HID]
                    last = v.tensor_scalar(yy, yy, sti["negm"][:, tc:tc + 1], None, ALU.add)
                return last
            P.stage("vector", f_lni1, reads=("yb", "st_li"), writes=("yb",))

            def f_lni2(v):
                last = None
                for tc in range(SUB):
                    yy = yb[:, tc * HID:(tc + 1) * HID]
                    last = v.scalar_tensor_tensor(yy, yy, sti["is"][:, tc:tc + 1], lnscb[:],
                                                  ALU.mult, ALU.mult)
                return last
            P.stage("vector", f_lni2, reads=("yb", "st_li", "lnscb"), writes=("yb",))

            def f_lni3(v):
                last = None
                for tc in range(SUB):
                    yy = yb[:, tc * HID:(tc + 1) * HID]
                    last = v.scalar_tensor_tensor(yy, yy, 0.0, lnbib[:], ALU.bypass, ALU.add,
                                                  accum_out=st2["sum"][:, tc:tc + 1])
                return last
            P.stage("vector", f_lni3, reads=("yb", "lnbib"), writes=("yb", "st_l2"))

            def f_lni4(v):
                last = None
                for tc in range(SUB):
                    last = v.scalar_tensor_tensor(
                        sqb[:], yb[:, tc * HID:(tc + 1) * HID], 0.0,
                        yb[:, tc * HID:(tc + 1) * HID],
                        ALU.bypass, ALU.mult, accum_out=st2["sq"][:, tc:tc + 1])
                return last
            P.stage("vector", f_lni4, reads=("yb",), writes=("sqb", "st_l2"))
            ln_tiny(st2, "st_l2", EPS1)
            yield
            st3 = stats["l3"]

            def f_b1(v):
                last = None
                for tc in range(SUB):
                    yy = yb[:, tc * HID:(tc + 1) * HID]
                    last = v.tensor_scalar(yy, yy, st2["negm"][:, tc:tc + 1],
                                           st2["is"][:, tc:tc + 1], ALU.add, ALU.mult)
                return last
            P.stage("vector", f_b1, reads=("yb", "st_l2"), writes=("yb",))

            def f_b2(v):
                last = None
                for tc in range(SUB):
                    last = v.scalar_tensor_tensor(
                        x2b[slot][:, tc * HID:(tc + 1) * HID], yb[:, tc * HID:(tc + 1) * HID],
                        0.0, xs[:, tc * HID:(tc + 1) * HID], ALU.bypass, ALU.add,
                        accum_out=st3["sum"][:, tc:tc + 1])
                return last
            P.stage("vector", f_b2, reads=("yb", xr), writes=(f"x2b{slot}", "st_l3"))

            def f_b3(v):
                last = None
                for tc in range(SUB):
                    xx2 = x2b[slot][:, tc * HID:(tc + 1) * HID]
                    last = v.scalar_tensor_tensor(sqb[:], xx2, 0.0, xx2, ALU.bypass, ALU.mult,
                                                  accum_out=st3["sq"][:, tc:tc + 1])
                return last
            P.stage("vector", f_b3, reads=(f"x2b{slot}",), writes=("sqb", "st_l3"))
            prefetch_for(NT + ti)
            ln_tiny(st3, "st_l3", EPS1)
            yield

            def f_ap3(v):
                last = None
                for tc in range(SUB):
                    last = v.tensor_scalar(
                        ln3b[:, tc * HID:(tc + 1) * HID], x2b[slot][:, tc * HID:(tc + 1) * HID],
                        st3["negm"][:, tc:tc + 1], st3["is"][:, tc:tc + 1], ALU.add, ALU.mult)
                return last
            P.stage("vector", f_ap3, reads=(f"x2b{slot}", "st_l3"), writes=("ln3b",))
            emit_trans(ln3b, "ln3b", xTf[slot], f"xTf{slot}")

        def ffn_slab_gen(ti):
            slot = ti % 2
            first_cp = True

            def emit_wl(q):
                wslot = q % 3

                def f_wl(sy, q=q, wslot=wslot):
                    src = ws_d[:, 4 * q:4 * (q + 1), :]
                    dst = wsb[wslot].ap().rearrange("p (k c) -> p k c", k=4)
                    return sy.dma_start(dst, src)
                extra = ()
                if ti == 0 and q < 3:
                    extra = [("PO0", P.cnt.get("PO0", 0)), ("PO1", P.cnt.get("PO1", 0)),
                             ("CE", P.cnt.get("CE", 0))]
                P.stage("sync", f_wl, writes=(f"wsb{wslot}",), inc=(f"LW{wslot}", 16),
                        extra=extra)

            def emit_z(sp):
                s0, zsl = 2 * sp, sp % 2

                def f_z(t, s0=s0, zsl=zsl):
                    last = None
                    for ds in range(2):
                        s = s0 + ds
                        wslot_ = (s // 4) % 3
                        rc = (s // 4) % 8
                        coff = (s % 4) * WSF + 1024
                        last = t.matmul(
                            ps_z[:, zsl * 512 + ds * 256: zsl * 512 + (ds + 1) * 256],
                            wsb[wslot_][:, coff:coff + 128],
                            xTf[slot][:, rc * TT:(rc + 1) * TT], start=True, stop=True)
                    return last
                P.stage("tensor", f_z, reads=(f"xTf{slot}", f"wsb{(s0 // 4) % 3}"),
                        writes=(f"ps_z{zsl}",))

                def f_gel(sc, sp=sp, zsl=zsl):
                    return sc.activation(hb[sp % 2][:], ps_z[:, zsl * 512:(zsl + 1) * 512],
                                         ACT.Gelu)
                P.stage("scalar", f_gel, reads=(f"ps_z{zsl}",), writes=(f"hb{sp % 2}",))

            emit_wl(0)
            emit_wl(1)
            emit_z(0)
            for sp in range(NSLAB // 2):
                s0 = 2 * sp
                if s0 % 4 == 0 and s0 // 4 + 2 < NSLAB // 4:
                    emit_wl(s0 // 4 + 2)
                if sp + 1 < NSLAB // 2:
                    emit_z(sp + 1)

                def f_cp(t, s0=s0, sp=sp):
                    last = None
                    for ds in range(2):
                        s = s0 + ds
                        wslot_ = (s // 4) % 3
                        woff = (s % 4) * WSF
                        for tc in range(SUB):
                            ps = ps_a if tc == 0 else ps_b
                            for half in range(2):
                                last = t.matmul(
                                    ps[:, half * 512:(half + 1) * 512],
                                    hb[sp % 2][:, ds * 256 + tc * 128: ds * 256 + (tc + 1) * 128],
                                    wsb[wslot_][:, woff + half * 512: woff + (half + 1) * 512],
                                    start=(s == 0), stop=(s == NSLAB - 1))
                    return last
                P.stage("tensor", f_cp, reads=(f"hb{sp % 2}", f"wsb{(s0 // 4) % 3}"),
                        writes=("ps_a", "ps_b") if first_cp else ())
                first_cp = False
                if sp % 4 == 3 and sp < NSLAB // 2 - 1:
                    yield

            st4 = stats["l4"]
            P.writer["ps_a"] = ("T", P.cnt["T"])
            P.writer["ps_b"] = ("T", P.cnt["T"])
            P.readers["ps_a"] = []
            P.readers["ps_b"] = []
            for tc in range(SUB):
                ps, psr = (ps_a, "ps_a") if tc == 0 else (ps_b, "ps_b")

                def f_ce(v, tc=tc, ps=ps):
                    return v.tensor_scalar(ffnyb[:, tc * HID:(tc + 1) * HID], ps[:, 0:HID],
                                           0.0, 0.0, ALU.add, ALU.add,
                                           accum_out=st4["sum"][:, tc:tc + 1])
                P.stage("vector", f_ce, reads=(psr,), writes=("ffnyb", "st_l4"))

        def ffn_tail(ti):
            slot = ti % 2
            st4 = stats["l4"]

            def f_sq4(v):
                last = None
                for tc in range(SUB):
                    last = v.scalar_tensor_tensor(
                        sqb[:], ffnyb[:, tc * HID:(tc + 1) * HID], 0.0,
                        ffnyb[:, tc * HID:(tc + 1) * HID], ALU.bypass, ALU.mult,
                        accum_out=st4["sq"][:, tc:tc + 1])
                return last
            P.stage("vector", f_sq4, reads=("ffnyb",), writes=("sqb", "st_l4"))
            ln_tiny(st4, "st_l4", EPS1)
            ob, obr = outb[ti % 2], f"outb{ti % 2}"

            def f_a4(v):
                last = None
                for tc in range(SUB):
                    last = v.tensor_scalar(ffnyb[:, tc * HID:(tc + 1) * HID],
                                           ffnyb[:, tc * HID:(tc + 1) * HID],
                                           st4["negm"][:, tc:tc + 1], st4["is"][:, tc:tc + 1],
                                           ALU.add, ALU.mult)
                return last
            P.stage("vector", f_a4, reads=("ffnyb", "st_l4"), writes=("ffnyb",))
            P.stage("vector", lambda v: v.tensor_tensor(ob[:], ffnyb[:], x2b[slot][:], ALU.add),
                    reads=("ffnyb", f"x2b{slot}"), writes=(obr,))

            def f_out(sy, ti=ti, ob=ob):
                src = ob.ap().rearrange("p (s f) -> p s f", s=SUB)
                return sy.dma_start(y_t[ti], src)
            P.stage("sync", f_out, reads=(obr,), inc=(f"OU{ti % 2}", 16))

        def exhaust(g):
            for _ in g:
                pass

        for ti in range(n_pre):
            for _ in front_gen(ti, False):
                if do_prepass:
                    prepass_step()
                    prepass_step()
        if do_prepass:
            while pp_i[0] < 2 * NSLAB:
                prepass_step()

        P.stage("vector", lambda v: v.tensor_scalar(carry[:], carry[:], pmaskb[:, 0:1], None,
                                                    ALU.mult),
                reads=("carry", "pmaskb"), writes=("carry",))

        bufmap = {}

        dbg_res = {"x2b": "x2b1", "l1sum": "st_l1", "l1sq": "st_l1", "l1negm": "st_l1", "l1var": "st_l1",
                   "l1is": "st_l1", "lisum": "st_li", "liis": "st_li", "l4is": "st_l4"}

        def emit_dbg():
            for nm, shape, dt in dbg_specs:
                buf = bufmap[nm]
                def f_d(sy, nm=nm, buf=buf):
                    return sy.dma_start(dbg_d[nm], buf[:])
                P.stage("sync", f_d, reads=(dbg_res.get(nm, nm),), inc=("OU0", 16))

        for ti in range(n_own):
            fg = front_gen(ti, True)
            if ti == 0:
                exhaust(fg)
            else:
                zg = ffn_slab_gen(ti - 1)
                done_f = done_z = False
                while not (done_f and done_z):
                    if not done_z:
                        done_z = next(zg, "END") == "END"
                    if not done_f:
                        done_f = next(fg, "END") == "END"
                ffn_tail(ti - 1)
                if debug and ti == 1:
                    bufmap.update(dict(xTb=xTa, qf=qf, kf=kf, vf=vf, attnb=attnb, qku=qku,
                                       innb=innb, yb=yb, x2b=x2b[1], ln1b=ln1b, ffnyb=ffnyb,
                                       carry=carry, gam=gam,
                                       l1sum=stats["l1"]["sum"], l1sq=stats["l1"]["sq"],
                                       l1negm=stats["l1"]["negm"], l1var=stats["l1"]["var"],
                                       l1is=stats["l1"]["is"], lisum=stats["li"]["sum"],
                                       liis=stats["li"]["is"], l4is=stats["l4"]["is"],
                                       gt=gt, stf=stf))
                    emit_dbg()
        if n_own > 0:
            exhaust(ffn_slab_gen(n_own - 1))
            ffn_tail(n_own - 1)

        P.barrier_wait("sync", [("OU0", P.cnt.get("OU0", 0)), ("OU1", P.cnt.get("OU1", 0))])

        with nc.Block() as block:
            def mk(eng_name):
                drain_eng = eng_name in ("vector", "scalar")

                def body(eng):
                    for waits, fn, inc in P.stream[eng_name]:
                        for smn, val in waits:
                            eng.wait_ge(sems[smn], val)
                        if fn is None:
                            continue
                        last = fn(eng)
                        smn, amt = inc
                        if drain_eng and smn in ("V", "S"):
                            last = eng.drain()
                        last.then_inc(sems[smn], amt)
                return body

            block.tensor(mk("tensor"))
            block.vector(mk("vector"))
            block.scalar(mk("scalar"))
            block.gpsimd(mk("gpsimd"))
            block.sync(mk("sync"))

    return nc


_NC_CACHE = None


def kernel(**inputs):
    global _NC_CACHE
    x = np.asarray(inputs["L0_attn_x"], dtype=np.float32)
    Wq = np.asarray(inputs["L0_attn_Wq"], dtype=np.float32)
    bq = np.asarray(inputs["L0_attn_bq"], dtype=np.float32)
    Wk = np.asarray(inputs["L0_attn_Wk"], dtype=np.float32)
    bk = np.asarray(inputs["L0_attn_bk"], dtype=np.float32)
    Wv = np.asarray(inputs["L0_attn_Wv"], dtype=np.float32)
    bv = np.asarray(inputs["L0_attn_bv"], dtype=np.float32)
    Wo = np.asarray(inputs["L0_attn_Wo"], dtype=np.float32)
    bo = np.asarray(inputs["L0_attn_bo"], dtype=np.float32)
    alpha = np.asarray(inputs["L0_attn_alpha"], dtype=np.float32)
    augp = np.asarray(inputs["L0_attn_aug_param"], dtype=np.float32)
    lns = np.asarray(inputs["L0_attn_ln_scale"], dtype=np.float32)
    lnb = np.asarray(inputs["L0_attn_ln_bias"], dtype=np.float32)
    rm = np.asarray(inputs["L0_ffn_rail_maker"], dtype=np.float32)
    up = np.asarray(inputs["L0_ffn_rail_up_proj"], dtype=np.float32)
    wc = np.asarray(inputs["L0_ffn_w_cproj"], dtype=np.float32)

    wq_h = np.ascontiguousarray(Wq.reshape(NFC, 128, NOC, 128).transpose(1, 0, 2, 3)
                                .reshape(128, NFC * NOC * 128))
    wk_h = np.ascontiguousarray(Wk.reshape(NFC, 128, NOC, 128).transpose(1, 0, 2, 3)
                                .reshape(128, NFC * NOC * 128))
    wv_h = np.ascontiguousarray(Wv.reshape(NFC, 128, NOC, 128).transpose(1, 0, 2, 3)
                                .reshape(128, NFC * NOC * 128))
    wo_h = np.ascontiguousarray(Wo.reshape(NOC, 128, HID).transpose(1, 0, 2)
                                .reshape(128, NOC * HID))
    wt_h = np.ascontiguousarray(wc.T)

    rmbd = np.zeros((32, 128, 128), np.float32)
    upbd = np.zeros((32, 128, 512), np.float32)
    for h in range(4):
        for rc in range(8):
            j = h * 8 + rc
            for r8 in range(8):
                r = rc * 8 + r8
                rmbd[j, r8 * 16:(r8 + 1) * 16, r8 * 16:(r8 + 1) * 16] = rm[h, r].T
                upbd[j, r8 * 16:(r8 + 1) * 16, r8 * 64:(r8 + 1) * 64] = up[h, r].T

    heads_of = (np.arange(INNER) // HD)
    sel_h = np.zeros((NOC, 128, 8), np.float32)
    selt_h = np.zeros((NOC, 8, 128), np.float32)
    for oc in range(NOC):
        for p in range(128):
            hh = heads_of[oc * 128 + p]
            sel_h[oc, p, hh] = float(HD) ** -0.5
            selt_h[oc, hh, p] = 1.0
    sel_h = np.ascontiguousarray(sel_h.transpose(1, 0, 2).reshape(128, NOC * 8))
    selt_h = np.ascontiguousarray(selt_h.transpose(1, 0, 2).reshape(8, NOC * 128))

    col = lambda a: np.ascontiguousarray(a.reshape(NOC, 128).T)
    shared = {
        "wq": wq_h, "wk": wk_h, "wv": wv_h, "wo": wo_h, "wt": wt_h,
        "rmbd": rmbd, "upbd": upbd, "sel": sel_h, "selt": selt_h,
        "avg8": np.full((8, 8), -0.125, np.float32), "one88": np.ones((8, 8), np.float32),
        "lnsc": np.ascontiguousarray(np.broadcast_to(lns, (128, HID))),
        "lnbi": np.ascontiguousarray(np.broadcast_to(lnb, (128, HID))),
        "bo": np.ascontiguousarray(np.broadcast_to(bo, (128, HID))),
        "bq": col(bq), "bk": col(bk), "bv": col(bv),
        "cvec": np.ascontiguousarray(np.broadcast_to(
            np.array([EPS1, EPSI, -0.5, -1.0, 0.5, 1.0], np.float32), (128, 6))),
        "alphac": col(alpha.reshape(INNER)), "augpc": col(augp.reshape(INNER)),
    }

    zeros_pre = np.zeros((TH, HID), np.float32)
    ones_m = np.ones((128, 1), np.float32)
    zeros_m = np.zeros((128, 1), np.float32)

    in_maps = []
    for c in range(8):
        b, hf = c // 2, c % 2
        m = dict(shared)
        m["x_own"] = np.ascontiguousarray(x[b, hf * TH:(hf + 1) * TH])
        m["x_pre"] = np.ascontiguousarray(x[b, 0:TH]) if hf else zeros_pre
        m["pmask"] = ones_m if hf else zeros_m
        in_maps.append(m)

    if _NC_CACHE is None:
        _NC_CACHE = build_nc()
    res = run_bass_kernel_spmd(_NC_CACHE, in_maps, core_ids=list(range(8)))

    out = np.empty((NB, SEQ, HID), np.float32)
    for c in range(8):
        b, hf = c // 2, c % 2
        out[b, hf * TH:(hf + 1) * TH] = res.results[c]["y"]
    return out


if __name__ == "__main__":
    print("building...")
    nc = build_nc()
    n_inst = 0
    try:
        n_inst = len(list(nc.all_instructions()))
    except Exception as e:
        print("count failed:", e)
    print("built, instructions:", n_inst)


# version 13 (final)
# speedup vs baseline: 1.0536x; 1.0164x over previous
import numpy as np
import concourse.bass as bass
import concourse.mybir as mybir
from concourse.bass_utils import run_bass_kernel_spmd
from contextlib import ExitStack

F32 = mybir.dt.float32
BF16 = mybir.dt.bfloat16
ALU = mybir.AluOpType
ACT = mybir.ActivationFunctionType

HID = 1024
INNER = 512
HD = 64
NB = 4
SEQ = 8192
TH = 4096
TT = 256
SUB = TT // 128
NT = TH // TT
NOC = INNER // 128
NFC = HID // 128
NSLAB = 128
WSF = 1152
EPS1 = 1e-5
EPSI = 1e-6

ENGS = ("tensor", "vector", "scalar", "gpsimd", "sync")
TILE_PHASE = 99


class Prog:

    def __init__(self):
        self.stream = {e: [] for e in ENGS}
        self.cnt = {}
        self.lastw = {e: {} for e in ENGS}
        self.writer = {}
        self.readers = {}
        self.eng_sem = {"tensor": "T", "vector": "V", "scalar": "S"}

    def _want(self, eng, waits, sem, val):
        if val <= 0:
            return
        if sem == self.eng_sem.get(eng):
            return
        if val > self.lastw[eng].get(sem, 0):
            self.lastw[eng][sem] = val
            waits.append((sem, val))

    def stage(self, eng, fn, reads=(), writes=(), inc=None, extra=()):
        waits = []
        for sem, val in extra:
            self._want(eng, waits, sem, val)
        for r in reads:
            w = self.writer.get(r)
            if w:
                self._want(eng, waits, *w)
        for r in writes:
            w = self.writer.get(r)
            if w:
                self._want(eng, waits, *w)
            for rd in self.readers.get(r, ()):
                self._want(eng, waits, *rd)
        if inc is None:
            sem, amt = self.eng_sem[eng], 1
        else:
            sem, amt = inc
        self.cnt[sem] = self.cnt.get(sem, 0) + amt
        mark = (sem, self.cnt[sem])
        for r in reads:
            self.readers.setdefault(r, []).append(mark)
        for r in writes:
            self.writer[r] = mark
            self.readers[r] = []
        self.stream[eng].append((waits, fn, (sem, amt)))
        return mark

    def barrier_wait(self, eng, marks):
        waits = []
        for sem, val in marks:
            self._want(eng, waits, sem, val)
        self.stream[eng].append((waits, None, None))


def build_nc(n_pre=NT, n_own=NT, do_cbuild=True, do_prepass=True, debug=False):
    nc = bass.Bass("TRN2", target_bir_lowering=False, debug=False,
                   detect_race_conditions=False)

    x_own = nc.dram_tensor("x_own", [TH, HID], F32, kind="ExternalInput").ap()
    x_pre = nc.dram_tensor("x_pre", [TH, HID], F32, kind="ExternalInput").ap()
    wq_d = nc.dram_tensor("wq", [128, NFC * NOC * 128], F32, kind="ExternalInput").ap()
    wk_d = nc.dram_tensor("wk", [128, NFC * NOC * 128], F32, kind="ExternalInput").ap()
    wv_d = nc.dram_tensor("wv", [128, NFC * NOC * 128], F32, kind="ExternalInput").ap()
    wo_d = nc.dram_tensor("wo", [128, NOC * HID], F32, kind="ExternalInput").ap()
    wt_d = nc.dram_tensor("wt", [HID * 16, HID], F32, kind="ExternalInput").ap()
    rmbd_d = nc.dram_tensor("rmbd", [32, 128, 128], F32, kind="ExternalInput").ap()
    upbd_d = nc.dram_tensor("upbd", [32, 128, 512], F32, kind="ExternalInput").ap()
    sel_d = nc.dram_tensor("sel", [128, NOC * 8], F32, kind="ExternalInput").ap()
    selt_d = nc.dram_tensor("selt", [8, NOC * 128], F32, kind="ExternalInput").ap()
    avg8_d = nc.dram_tensor("avg8", [8, 8], F32, kind="ExternalInput").ap()
    one88_d = nc.dram_tensor("one88", [8, 8], F32, kind="ExternalInput").ap()
    lnsc_d = nc.dram_tensor("lnsc", [128, HID], F32, kind="ExternalInput").ap()
    lnbi_d = nc.dram_tensor("lnbi", [128, HID], F32, kind="ExternalInput").ap()
    bo_d = nc.dram_tensor("bo", [128, HID], F32, kind="ExternalInput").ap()
    bq_d = nc.dram_tensor("bq", [128, NOC], F32, kind="ExternalInput").ap()
    bk_d = nc.dram_tensor("bk", [128, NOC], F32, kind="ExternalInput").ap()
    bv_d = nc.dram_tensor("bv", [128, NOC], F32, kind="ExternalInput").ap()
    alpha_d = nc.dram_tensor("alphac", [128, NOC], F32, kind="ExternalInput").ap()
    augp_d = nc.dram_tensor("augpc", [128, NOC], F32, kind="ExternalInput").ap()
    pmask_d = nc.dram_tensor("pmask", [128, 1], F32, kind="ExternalInput").ap()
    cvec_d = nc.dram_tensor("cvec", [128, 6], F32, kind="ExternalInput").ap()
    y_d = nc.dram_tensor("y", [TH, HID], F32, kind="ExternalOutput").ap()

    ws_d = nc.dram_tensor("ws", [128, NSLAB, WSF], BF16, kind="Internal").ap()
    dbg_specs = [("xTb", [128, NFC * TT], BF16), ("qf", [128, NOC * TT], F32),
                 ("kf", [128, NOC * TT], F32), ("vf", [128, NOC * TT], F32),
                 ("attnb", [8, TT], F32), ("qku", [128, NOC * TT], F32),
                 ("innb", [128, NOC * TT], BF16), ("yb", [128, SUB * HID], F32),
                 ("x2b", [128, SUB * HID], F32), ("ln1b", [128, SUB * HID], BF16),
                 ("ffnyb", [128, SUB * HID], F32), ("carry", [128, NOC], F32),
                 ("gam", [128, NOC], F32),
                 ("l1sum", [128, 2], F32), ("l1sq", [128, 2], F32),
                 ("l1negm", [128, 2], F32), ("l1var", [128, 2], F32),
                 ("l1is", [128, 2], F32), ("lisum", [128, 2], F32),
                 ("liis", [128, 2], F32), ("l4is", [128, 2], F32),
                 ("gt", [128, NOC * TT], F32), ("stf", [128, NOC * TT], F32)]
    dbg_d = {}
    if debug:
        for nm, shape, dt in dbg_specs:
            dbg_d[nm] = nc.dram_tensor("dbg_" + nm, shape, dt, kind="ExternalOutput").ap()

    x_own_t = x_own.rearrange("(t s p) f -> t p s f", p=128, s=SUB)
    x_pre_t = x_pre.rearrange("(t s p) f -> t p s f", p=128, s=SUB)
    y_t = y_d.rearrange("(t s p) f -> t p s f", p=128, s=SUB)

    ctx = ExitStack()
    sb = lambda n, shape, dt=F32: ctx.enter_context(nc.sbuf_tensor(n, shape, dt))
    psb = lambda n, shape, dt=F32: ctx.enter_context(nc.psum_tensor(n, shape, dt))

    xb = [sb(f"xb{j}", [128, SUB * HID]) for j in range(2)]
    ln1b = sb("ln1b", [128, SUB * HID], BF16)
    ln3b = sb("ln3b", [128, SUB * HID], BF16)
    xTa = sb("xTa", [128, NFC * TT], BF16)
    xTf = [sb(f"xTf{j}", [128, NFC * TT], BF16) for j in range(2)]
    qf = sb("qf", [128, NOC * TT])
    kf = sb("kf", [128, NOC * TT])
    vf = sb("vf", [128, NOC * TT])
    stf = sb("stf", [128, NOC * TT])
    qku = sb("qku", [128, NOC * TT])
    gt = sb("gt", [128, NOC * TT])
    tanb = sb("tanb", [128, NOC * TT])
    innb = sb("innb", [128, NOC * TT], BF16)
    yb = sb("yb", [128, SUB * HID])
    x2b = [sb(f"x2b{j}", [128, SUB * HID]) for j in range(2)]
    ffnyb = sb("ffnyb", [128, SUB * HID])
    outb = [sb(f"outb{j}", [128, SUB * HID]) for j in range(2)]
    sqb = sb("sqb", [128, HID])
    carry = sb("carry", [128, NOC])
    wqb = sb("wqb", [128, NFC * NOC * 128], BF16)
    wkb = sb("wkb", [128, NFC * NOC * 128], BF16)
    wvb = sb("wvb", [128, NFC * NOC * 128], BF16)
    wob = sb("wob", [128, NOC * HID], BF16)
    wsb = [sb(f"wsb{j}", [128, 4 * WSF], BF16) for j in range(3)]
    hb = [sb(f"hb{j}", [128, 512], BF16) for j in range(2)]
    selb = sb("selb", [128, NOC * 8])
    seltb = sb("seltb", [8, NOC * 128])
    avg8b = sb("avg8b", [8, 8])
    one88b = sb("one88b", [8, 8])
    lnscb = sb("lnscb", [128, HID])
    lnbib = sb("lnbib", [128, HID])
    bob = sb("bob", [128, HID])
    bqc = sb("bqc", [128, NOC])
    bkc = sb("bkc", [128, NOC])
    bvc = sb("bvc", [128, NOC])
    alphac = sb("alphacS", [128, NOC])
    augph = sb("augph", [128, NOC])
    gam = sb("gam", [128, NOC])
    omg = sb("omg", [128, NOC])
    pmaskb = sb("pmaskb", [128, 1])
    cvecb = sb("cvecb", [128, 6])
    cds = [sb(f"cd{i}", [128, 1]) for i in range(5)]
    bqd = [sb(f"bqd{i}", [128, 1]) for i in range(NOC)]
    bkd = [sb(f"bkd{i}", [128, 1]) for i in range(NOC)]
    augd = [sb(f"augd{i}", [128, 1]) for i in range(NOC)]
    lrow = sb("lrow", [8, TT])
    einb = sb("einb", [8, TT])
    erow = sb("erow", [8, TT])
    attnb = sb("attnb", [8, TT])
    lnz8 = sb("lnz8", [8, TT])
    rz8 = sb("rz8", [8, TT])
    stats = {k: {r: sb(f"st_{k}_{r}", [128, 2]) for r in ("sum", "sq", "negm", "var", "is", "msq", "lnv")}
             for k in ("l1", "li", "l2", "l3", "l4")}
    ppin = [sb(f"ppin{j}", [128, 512]) for j in range(2)]
    ppout = [sb(f"ppout{j}", [128, 512], BF16) for j in range(2)]

    ps_a = psb("ps_a", [128, 1024])
    ps_b = psb("ps_b", [128, 1024])
    ps_q = psb("ps_q", [128, 1024])
    ps_z = psb("ps_z", [128, 1024])

    sem_names = ["LX0", "LX1", "LC", "CR0", "CR1", "CU0", "CU1", "CE",
                 "PI0", "PI1", "PO0", "PO1", "OU0", "OU1",
                 "LW0", "LW1", "LW2", "TR", "T", "V", "S"]
    sems = {n: ctx.enter_context(nc.semaphore(n)) for n in sem_names}

    P = Prog()

    with ctx:
        consts = [
            (wqb[:], wq_d, "wqb"), (wkb[:], wk_d, "wkb"), (wvb[:], wv_d, "wvb"),
            (wob[:], wo_d, "wob"), (selb[:], sel_d, "selb"), (seltb[:], selt_d, "seltb"),
            (avg8b[:], avg8_d, "avg8b"), (one88b[:], one88_d, "one88b"),
            (lnscb[:], lnsc_d, "lnscb"),
            (lnbib[:], lnbi_d, "lnbib"), (bob[:], bo_d, "bob"),
            (bqc[:], bq_d, "bqc"), (bkc[:], bk_d, "bkc"), (bvc[:], bv_d, "bvc"),
            (alphac[:], alpha_d, "alphac"), (augph[:], augp_d, "augph"),
            (pmaskb[:], pmask_d, "pmaskb"),
            (cvecb[:], cvec_d, "cvecb"),
        ]
        for j, (dst, src, res) in enumerate(consts):
            if j % 4 == 0 and j > 0:
                w = 16 * j
                if w > P.lastw["gpsimd"].get("LC", 0):
                    P.lastw["gpsimd"]["LC"] = w
                    P.stream["gpsimd"].append(([("LC", w)], None, None))

            def f_ld(g, dst=dst, src=src):
                return g.dma_start(dst, src)
            P.stage("gpsimd", f_ld, writes=(res,), inc=("LC", 16))
        lc_total = ("LC", P.cnt["LC"])
        for _, _, res in consts:
            P.writer[res] = lc_total

        def xload(ti, src_t, slot):
            def f(g, ti=ti, src_t=src_t, slot=slot):
                dst = xb[slot].ap().rearrange("p (s f) -> p s f", s=SUB)
                return g.dma_start(dst, src_t[ti])
            return P.stage("gpsimd", f, writes=(f"xb{slot}",), inc=(f"LX{slot}", 16))

        if n_pre > 0:
            xload(0, x_pre_t, 0)
        if n_pre > 1:
            xload(1, x_pre_t, 1)
        if n_pre == 0 and n_own > 0:
            xload(0, x_own_t, 0)
        if n_pre == 0 and n_own > 1:
            xload(1, x_own_t, 1)

        for i in range(5):
            def f_cd(v, i=i):
                return v.tensor_copy(cds[i][:], cvecb[:, i:i + 1])
            P.stage("vector", f_cd, reads=("cvecb",), writes=(f"cd{i}",))
        for oc in range(NOC):
            def f_bq(v, oc=oc):
                return v.tensor_copy(bqd[oc][:], bqc[:, oc:oc + 1])
            P.stage("vector", f_bq, reads=("bqc",), writes=(f"bqd{oc}",))
            def f_bk(v, oc=oc):
                return v.tensor_copy(bkd[oc][:], bkc[:, oc:oc + 1])
            P.stage("vector", f_bk, reads=("bkc",), writes=(f"bkd{oc}",))
            def f_ag(v, oc=oc):
                return v.tensor_scalar(augd[oc][:], augph[:, oc:oc + 1], 0.5, None, ALU.mult)
            P.stage("vector", f_ag, reads=("augph",), writes=(f"augd{oc}",))

        P.stage("scalar", lambda s: s.activation(gam[:], alphac[:], ACT.Tanh,
                                                 scale=cds[4][:]),
                reads=("alphac", "cd4"), writes=("gam",))
        P.stage("vector", lambda v: v.tensor_scalar(gam[:], gam[:], 0.5, 0.5, ALU.mult, ALU.add),
                reads=("gam",), writes=("gam",))
        P.stage("vector", lambda v: v.tensor_scalar(omg[:], gam[:], -1.0, 1.0, ALU.mult, ALU.add),
                reads=("gam",), writes=("omg",))
        P.stage("vector", lambda v: v.memset(gt[:], 0.0), writes=("gt",))
        for oc in range(NOC):
            def f_gt(v, oc=oc):
                return v.tensor_scalar(gt[:, oc * TT:(oc + 1) * TT], gt[:, oc * TT:(oc + 1) * TT],
                                       gam[:, oc:oc + 1], None, ALU.add)
            P.stage("vector", f_gt, reads=("gam", "gt"), writes=("gt",))
        P.stage("vector", lambda v: v.memset(carry[:], 0.0), writes=("carry",))

        rmbb = ffnyb[:, 0:128]
        upbb = ffnyb[:, 128:640]
        cevb = innb[:, 0:512]
        for j in range(32 if do_cbuild else 0):
            def f_lr(sy, j=j):
                return sy.dma_start(rmbb, rmbd_d[j])
            P.stage("sync", f_lr, writes=("rmbb0", "ffnyb"), inc=("CR0", 16))

            def f_lu(sy, j=j):
                return sy.dma_start(upbb, upbd_d[j])
            P.stage("sync", f_lu, writes=("upbb0", "ffnyb"), inc=("CU0", 16))

            def f_mm(t):
                return t.matmul(ps_a[:, 0:512], rmbb, upbb, start=True, stop=True)
            P.stage("tensor", f_mm, reads=("rmbb0", "upbb0", "ffnyb"), writes=("ps_a",))

            P.stage("vector", lambda v: v.tensor_copy(cevb, ps_a[:, 0:512]),
                    reads=("ps_a",), writes=("cevb", "innb"))

            def f_st(sy, j=j):
                dst = ws_d[:, 4 * j:4 * (j + 1), 1024:1152]
                src2 = innb.ap()[:, 0:512].rearrange("p (k c) -> p k c", k=4)
                return sy.dma_start(dst, src2)
            P.stage("sync", f_st, reads=("cevb", "innb"), inc=("CE", 16))

        pp_i = [0]

        def prepass_step():
            c = pp_i[0]
            if c >= 2 * NSLAB:
                return
            pp_i[0] += 1
            slot = c % 2
            row, half = c // 2, c % 2

            def f_li(sy, row=row, half=half, slot=slot):
                return sy.dma_start(ppin[slot][:],
                                    wt_d[row * 128:(row + 1) * 128, half * 512:(half + 1) * 512])
            P.stage("sync", f_li, writes=(f"ppin{slot}",), inc=(f"PI{slot}", 16))

            def f_cast(sc, slot=slot):
                return sc.copy(ppout[slot][:], ppin[slot][:])
            P.stage("scalar", f_cast, reads=(f"ppin{slot}",), writes=(f"ppout{slot}",))

            def f_so(sy, row=row, half=half, slot=slot):
                return sy.dma_start(ws_d[:, row, half * 512:(half + 1) * 512], ppout[slot][:])
            P.stage("sync", f_so, reads=(f"ppout{slot}",), inc=(f"PO{slot}", 16))

        def ln_tiny(st, key, eps):
            def fv1(v):
                v.tensor_scalar(st["negm"][:], st["sum"][:], -1.0 / HID, None, ALU.mult)
                return v.tensor_scalar(st["var"][:], st["sq"][:], 1.0 / HID, None, ALU.mult)
            P.stage("vector", fv1, reads=(key,), writes=(key,))
            P.stage("vector", lambda v: v.tensor_tensor(st["msq"][:], st["negm"][:],
                                                        st["negm"][:], ALU.mult),
                    reads=(key,), writes=(key,))
            P.stage("vector", lambda v: v.tensor_tensor(st["var"][:], st["var"][:],
                                                        st["msq"][:], ALU.subtract),
                    reads=(key,), writes=(key,))
            epsc, epsr = (cds[0], "cd0") if eps == EPS1 else (cds[1], "cd1")

            def fs(sc, epsc=epsc):
                sc.activation(st["lnv"][:], st["var"][:], ACT.Ln, bias=epsc[:])
                return sc.activation(st["is"][:], st["lnv"][:], ACT.Exp, scale=cds[2][:])
            P.stage("scalar", fs, reads=(key, epsr, "cd2"), writes=(key,))

        def emit_trans(srcbuf, srcres, dstbuf, dstres):
            for s in range(SUB):
                for fc in range(NFC):
                    eng = "scalar" if (s * NFC + fc) % 2 == 0 else "sync"

                    def f_tr(e, s=s, fc=fc, srcbuf=srcbuf, dstbuf=dstbuf):
                        return e.dma_start(
                            dstbuf[:, fc * TT + s * 128: fc * TT + (s + 1) * 128],
                            srcbuf[:, s * HID + fc * 128: s * HID + (fc + 1) * 128],
                            transpose=True)
                    P.stage(eng, f_tr, reads=(srcres,), writes=(dstres,),
                            inc=("TR", 16))

        def prefetch_for(gidx):
            nxt = gidx + 2
            if nxt < NT + NT:
                is_own2 = nxt >= NT
                ok = (nxt - NT < n_own) if is_own2 else (nxt < n_pre)
                if ok:
                    xload(nxt - NT if is_own2 else nxt,
                          x_own_t if is_own2 else x_pre_t, nxt % 2)

        PSQ = ("ps_q", "ps_q_l", "ps_q_r")

        def front_gen(ti, own):
            slot = ti % 2
            gidx = (NT + ti) if own else ti
            st1 = stats["l1"]
            xr = f"xb{slot}"
            xs = xb[slot]

            def f_st1(v):
                for s in range(SUB):
                    v.scalar_tensor_tensor(
                        sqb[:], xs[:, s * HID:(s + 1) * HID], 0.0,
                        xs[:, s * HID:(s + 1) * HID],
                        ALU.bypass, ALU.mult, accum_out=st1["sq"][:, s:s + 1])
                return v.reduce_sum(st1["sum"][:], xs.ap().rearrange("p (s f) -> p s f", s=SUB),
                                    axis=mybir.AxisListType.X)
            P.stage("vector", f_st1, reads=(xr,), writes=("sqb", "st_l1"))
            yield
            ln_tiny(st1, "st_l1", EPS1)
            yield

            def f_ap(v):
                last = None
                for s in range(SUB):
                    last = v.tensor_scalar(
                        ln1b[:, s * HID:(s + 1) * HID], xs[:, s * HID:(s + 1) * HID],
                        st1["negm"][:, s:s + 1], st1["is"][:, s:s + 1], ALU.add, ALU.mult)
                return last
            P.stage("vector", f_ap, reads=(xr, "st_l1"), writes=("ln1b",))
            emit_trans(ln1b, "ln1b", xTa, "xTa")
            if not own:
                prefetch_for(gidx)
            yield

            def mk_qkv(wbuf, wres):
                def f(t, wbuf=wbuf):
                    last = None
                    for oc in range(NOC):
                        for fc in range(NFC):
                            last = t.matmul(
                                ps_q[:, oc * TT:(oc + 1) * TT],
                                wbuf[:, (fc * NOC + oc) * 128:(fc * NOC + oc + 1) * 128],
                                xTa[:, fc * TT:(fc + 1) * TT],
                                start=(fc == 0), stop=(fc == NFC - 1))
                    return last
                return f

            P.stage("tensor", mk_qkv(wqb, "wqb"), reads=("xTa", "wqb"), writes=PSQ)
            yield

            def f_qe(sc):
                last = None
                for oc in range(NOC):
                    last = sc.activation(qf[:, oc * TT:(oc + 1) * TT],
                                         ps_q[:, oc * TT:(oc + 1) * TT],
                                         ACT.Identity, bias=bqd[oc][:])
                return last
            P.stage("scalar", f_qe, reads=PSQ + ("bqd0", "bqd1", "bqd2", "bqd3"),
                    writes=("qf",))
            P.stage("tensor", mk_qkv(wkb, "wkb"), reads=("xTa", "wkb"), writes=PSQ)
            yield

            def f_ke(sc):
                last = None
                for oc in range(NOC):
                    last = sc.activation(kf[:, oc * TT:(oc + 1) * TT],
                                         ps_q[:, oc * TT:(oc + 1) * TT],
                                         ACT.Identity, bias=bkd[oc][:])
                return last
            P.stage("scalar", f_ke, reads=PSQ + ("bkd0", "bkd1", "bkd2", "bkd3"),
                    writes=("kf",))
            P.stage("vector", lambda v: v.tensor_tensor(qku[:], qf[:], kf[:], ALU.mult),
                    reads=("qf", "kf"), writes=("qku",))
            P.stage("tensor", mk_qkv(wvb, "wvb"), reads=("xTa", "wvb"), writes=PSQ)
            yield

            def f_ve(v):
                last = None
                for oc in range(NOC):
                    last = v.tensor_scalar(vf[:, oc * TT:(oc + 1) * TT],
                                           ps_q[:, oc * TT:(oc + 1) * TT],
                                           bvc[:, oc:oc + 1], None, ALU.add)
                return last
            P.stage("vector", f_ve, reads=PSQ + ("bvc",), writes=("vf",))

            def f_log(t):
                last = None
                for oc in range(NOC):
                    last = t.matmul(ps_q[0:8, 0:TT], selb[:, oc * 8:(oc + 1) * 8],
                                    qku[:, oc * TT:(oc + 1) * TT],
                                    start=(oc == 0), stop=(oc == NOC - 1))
                return last
            P.stage("tensor", f_log, reads=("qku", "selb", "vf"), writes=("ps_q_l",))
            yield
            P.stage("vector", lambda v: v.tensor_copy(lrow[:], ps_q[0:8, 0:TT]),
                    reads=("ps_q_l",), writes=("lrow",))
            P.stage("tensor", lambda t: t.matmul(ps_q[0:8, 256:512], avg8b[:], lrow[:],
                                                 start=True, stop=True),
                    reads=("lrow", "avg8b"), writes=("ps_q_r",))
            P.stage("vector", lambda v: v.tensor_tensor(einb[:], lrow[:],
                                                        ps_q[0:8, 256:512], ALU.add),
                    reads=("lrow", "ps_q_r"), writes=("einb",))
            P.stage("scalar", lambda sc: sc.activation(erow[:], einb[:], ACT.Exp),
                    reads=("einb",), writes=("erow",))
            yield
            P.stage("tensor", lambda t: t.matmul(ps_q[0:8, 256:512], one88b[:], erow[:],
                                                 start=True, stop=True),
                    reads=("erow", "one88b"), writes=("ps_q_r",))
            def f_rz(sc):
                sc.activation(lnz8[:], ps_q[0:8, 256:512], ACT.Ln)
                return sc.activation(rz8[:], lnz8[:], ACT.Exp, scale=cds[3][0:8, :])
            P.stage("scalar", f_rz, reads=("ps_q_r", "cd3"), writes=("lnz8", "rz8"))
            P.stage("vector", lambda v: v.tensor_tensor(attnb[:], erow[:], rz8[:], ALU.mult),
                    reads=("erow", "rz8"), writes=("attnb",))
            yield

            def f_bc(t):
                last = None
                for oc in range(NOC):
                    last = t.matmul(ps_q[:, oc * TT:(oc + 1) * TT],
                                    seltb[:, oc * 128:(oc + 1) * 128], attnb[:],
                                    start=True, stop=True)
                return last
            P.stage("tensor", f_bc, reads=("attnb", "seltb"), writes=PSQ)
            yield

            def f_u(v):
                last = None
                for oc in range(NOC):
                    last = v.scalar_tensor_tensor(
                        qku[:, oc * TT:(oc + 1) * TT], ps_q[:, oc * TT:(oc + 1) * TT],
                        omg[:, oc:oc + 1], vf[:, oc * TT:(oc + 1) * TT], ALU.mult, ALU.mult)
                return last
            P.stage("vector", f_u, reads=PSQ + ("omg", "vf", "qku"), writes=("qku",))

            def f_scan(v):
                last = None
                for oc in range(NOC):
                    last = v.tensor_tensor_scan(
                        stf[:, oc * TT:(oc + 1) * TT], gt[:, oc * TT:(oc + 1) * TT],
                        qku[:, oc * TT:(oc + 1) * TT], carry[:, oc:oc + 1], ALU.mult, ALU.add)
                return last
            P.stage("vector", f_scan, reads=("gt", "qku", "carry"), writes=("stf",))

            def f_cs(v):
                src = stf.ap().rearrange("p (o t) -> p o t", o=NOC)[:, :, TT - 1:TT]
                dst = carry.ap().rearrange("p (o c) -> p o c", c=1)
                return v.tensor_copy(dst, src)
            P.stage("vector", f_cs, reads=("stf",), writes=("carry",))
            if not own:
                return
            yield
            for _ in chain_tail(ti):
                yield

        def chain_tail(ti):
            slot = ti % 2
            xr = f"xb{slot}"
            xs = xb[slot]

            def f_aug(sc):
                last = None
                for oc in range(NOC):
                    last = sc.activation(tanb[:, oc * TT:(oc + 1) * TT],
                                         qf[:, oc * TT:(oc + 1) * TT],
                                         ACT.Tanh, scale=augd[oc][:])
                return last
            P.stage("scalar", f_aug, reads=("qf", "augd0", "augd1", "augd2", "augd3"),
                    writes=("tanb",))
            P.stage("vector", lambda v: v.tensor_tensor(stf[:], qf[:], stf[:], ALU.mult),
                    reads=("qf", "stf"), writes=("stf",))
            P.stage("vector", lambda v: v.scalar_tensor_tensor(stf[:], tanb[:], 0.5, stf[:],
                                                               ALU.mult, ALU.add),
                    reads=("tanb", "stf"), writes=("stf",))
            P.stage("vector", lambda v: v.tensor_scalar(innb[:], stf[:], 0.5, None, ALU.add),
                    reads=("stf",), writes=("innb",))
            yield

            sti = stats["li"]
            for tc in range(SUB):
                def f_op(t, tc=tc):
                    last = None
                    for half in range(2):
                        for oc in range(NOC):
                            last = t.matmul(
                                ps_q[:, half * 512:(half + 1) * 512],
                                innb[:, oc * TT + tc * 128: oc * TT + (tc + 1) * 128],
                                wob[:, oc * HID + half * 512: oc * HID + (half + 1) * 512],
                                start=(oc == 0), stop=(oc == NOC - 1))
                    return last
                P.stage("tensor", f_op, reads=("innb", "wob"), writes=PSQ)

                def f_ye(v, tc=tc):
                    return v.scalar_tensor_tensor(
                        yb[:, tc * HID:(tc + 1) * HID], ps_q[:, 0:HID], 0.0, bob[:],
                        ALU.bypass, ALU.add, accum_out=sti["sum"][:, tc:tc + 1])
                P.stage("vector", f_ye, reads=PSQ + ("bob",), writes=("yb", "st_li"))
                yield

            for tc in range(SUB):
                def f_sqi(v, tc=tc):
                    return v.scalar_tensor_tensor(
                        sqb[:], yb[:, tc * HID:(tc + 1) * HID], 0.0,
                        yb[:, tc * HID:(tc + 1) * HID],
                        ALU.bypass, ALU.mult, accum_out=sti["sq"][:, tc:tc + 1])
                P.stage("vector", f_sqi, reads=("yb",), writes=("sqb", "st_li"))
            ln_tiny(sti, "st_li", EPSI)
            yield
            st2 = stats["l2"]

            def f_lni1(v):
                last = None
                for tc in range(SUB):
                    yy = yb[:, tc * HID:(tc + 1) * HID]
                    last = v.tensor_scalar(yy, yy, sti["negm"][:, tc:tc + 1], None, ALU.add)
                return last
            P.stage("vector", f_lni1, reads=("yb", "st_li"), writes=("yb",))

            def f_lni2(v):
                last = None
                for tc in range(SUB):
                    yy = yb[:, tc * HID:(tc + 1) * HID]
                    last = v.scalar_tensor_tensor(yy, yy, sti["is"][:, tc:tc + 1], lnscb[:],
                                                  ALU.mult, ALU.mult)
                return last
            P.stage("vector", f_lni2, reads=("yb", "st_li", "lnscb"), writes=("yb",))

            def f_lni3(v):
                last = None
                for tc in range(SUB):
                    yy = yb[:, tc * HID:(tc + 1) * HID]
                    last = v.scalar_tensor_tensor(yy, yy, 0.0, lnbib[:], ALU.bypass, ALU.add,
                                                  accum_out=st2["sum"][:, tc:tc + 1])
                return last
            P.stage("vector", f_lni3, reads=("yb", "lnbib"), writes=("yb", "st_l2"))

            def f_lni4(v):
                last = None
                for tc in range(SUB):
                    last = v.scalar_tensor_tensor(
                        sqb[:], yb[:, tc * HID:(tc + 1) * HID], 0.0,
                        yb[:, tc * HID:(tc + 1) * HID],
                        ALU.bypass, ALU.mult, accum_out=st2["sq"][:, tc:tc + 1])
                return last
            P.stage("vector", f_lni4, reads=("yb",), writes=("sqb", "st_l2"))
            ln_tiny(st2, "st_l2", EPS1)
            yield
            st3 = stats["l3"]

            def f_b1(v):
                last = None
                for tc in range(SUB):
                    yy = yb[:, tc * HID:(tc + 1) * HID]
                    last = v.tensor_scalar(yy, yy, st2["negm"][:, tc:tc + 1],
                                           st2["is"][:, tc:tc + 1], ALU.add, ALU.mult)
                return last
            P.stage("vector", f_b1, reads=("yb", "st_l2"), writes=("yb",))

            def f_b2(v):
                last = None
                for tc in range(SUB):
                    last = v.scalar_tensor_tensor(
                        x2b[slot][:, tc * HID:(tc + 1) * HID], yb[:, tc * HID:(tc + 1) * HID],
                        0.0, xs[:, tc * HID:(tc + 1) * HID], ALU.bypass, ALU.add,
                        accum_out=st3["sum"][:, tc:tc + 1])
                return last
            P.stage("vector", f_b2, reads=("yb", xr), writes=(f"x2b{slot}", "st_l3"))

            def f_b3(v):
                last = None
                for tc in range(SUB):
                    xx2 = x2b[slot][:, tc * HID:(tc + 1) * HID]
                    last = v.scalar_tensor_tensor(sqb[:], xx2, 0.0, xx2, ALU.bypass, ALU.mult,
                                                  accum_out=st3["sq"][:, tc:tc + 1])
                return last
            P.stage("vector", f_b3, reads=(f"x2b{slot}",), writes=("sqb", "st_l3"))
            prefetch_for(NT + ti)
            ln_tiny(st3, "st_l3", EPS1)
            yield

            def f_ap3(v):
                last = None
                for tc in range(SUB):
                    last = v.tensor_scalar(
                        ln3b[:, tc * HID:(tc + 1) * HID], x2b[slot][:, tc * HID:(tc + 1) * HID],
                        st3["negm"][:, tc:tc + 1], st3["is"][:, tc:tc + 1], ALU.add, ALU.mult)
                return last
            P.stage("vector", f_ap3, reads=(f"x2b{slot}", "st_l3"), writes=("ln3b",))
            emit_trans(ln3b, "ln3b", xTf[slot], f"xTf{slot}")

        def ffn_slab_gen(ti):
            slot = ti % 2
            first_cp = True

            def emit_wl(q):
                wslot = q % 3

                def f_wl(sy, q=q, wslot=wslot):
                    src = ws_d[:, 4 * q:4 * (q + 1), :]
                    dst = wsb[wslot].ap().rearrange("p (k c) -> p k c", k=4)
                    return sy.dma_start(dst, src)
                extra = ()
                if ti == 0 and q < 3:
                    extra = [("PO0", P.cnt.get("PO0", 0)), ("PO1", P.cnt.get("PO1", 0)),
                             ("CE", P.cnt.get("CE", 0))]
                P.stage("sync", f_wl, writes=(f"wsb{wslot}",), inc=(f"LW{wslot}", 16),
                        extra=extra)

            def emit_z(sp):
                s0, zsl = 2 * sp, sp % 2

                def f_z(t, s0=s0, zsl=zsl):
                    last = None
                    for ds in range(2):
                        s = s0 + ds
                        wslot_ = (s // 4) % 3
                        rc = (s // 4) % 8
                        coff = (s % 4) * WSF + 1024
                        last = t.matmul(
                            ps_z[:, zsl * 512 + ds * 256: zsl * 512 + (ds + 1) * 256],
                            wsb[wslot_][:, coff:coff + 128],
                            xTf[slot][:, rc * TT:(rc + 1) * TT], start=True, stop=True)
                    return last
                P.stage("tensor", f_z, reads=(f"xTf{slot}", f"wsb{(s0 // 4) % 3}"),
                        writes=(f"ps_z{zsl}",))

                def f_gel(sc, sp=sp, zsl=zsl):
                    return sc.activation(hb[sp % 2][:], ps_z[:, zsl * 512:(zsl + 1) * 512],
                                         ACT.Gelu)
                P.stage("scalar", f_gel, reads=(f"ps_z{zsl}",), writes=(f"hb{sp % 2}",))

            emit_wl(0)
            emit_wl(1)
            emit_z(0)
            for sp in range(NSLAB // 2):
                s0 = 2 * sp
                if s0 % 4 == 0 and s0 // 4 + 2 < NSLAB // 4:
                    emit_wl(s0 // 4 + 2)
                if sp + 1 < NSLAB // 2:
                    emit_z(sp + 1)

                def f_cp(t, s0=s0, sp=sp):
                    last = None
                    for ds in range(2):
                        s = s0 + ds
                        wslot_ = (s // 4) % 3
                        woff = (s % 4) * WSF
                        for tc in range(SUB):
                            ps = ps_a if tc == 0 else ps_b
                            for half in range(2):
                                last = t.matmul(
                                    ps[:, half * 512:(half + 1) * 512],
                                    hb[sp % 2][:, ds * 256 + tc * 128: ds * 256 + (tc + 1) * 128],
                                    wsb[wslot_][:, woff + half * 512: woff + (half + 1) * 512],
                                    start=(s == 0), stop=(s == NSLAB - 1))
                    return last
                P.stage("tensor", f_cp, reads=(f"hb{sp % 2}", f"wsb{(s0 // 4) % 3}"),
                        writes=("ps_a", "ps_b") if first_cp else ())
                first_cp = False
                if sp % 3 == 2 and sp < NSLAB // 2 - 1:
                    yield

            st4 = stats["l4"]
            P.writer["ps_a"] = ("T", P.cnt["T"])
            P.writer["ps_b"] = ("T", P.cnt["T"])
            P.readers["ps_a"] = []
            P.readers["ps_b"] = []
            for tc in range(SUB):
                ps, psr = (ps_a, "ps_a") if tc == 0 else (ps_b, "ps_b")

                def f_ce(v, tc=tc, ps=ps):
                    return v.tensor_scalar(ffnyb[:, tc * HID:(tc + 1) * HID], ps[:, 0:HID],
                                           0.0, 0.0, ALU.add, ALU.add,
                                           accum_out=st4["sum"][:, tc:tc + 1])
                P.stage("vector", f_ce, reads=(psr,), writes=("ffnyb", "st_l4"))

        def ffn_tail(ti):
            slot = ti % 2
            st4 = stats["l4"]

            def f_sq4(v):
                last = None
                for tc in range(SUB):
                    last = v.scalar_tensor_tensor(
                        sqb[:], ffnyb[:, tc * HID:(tc + 1) * HID], 0.0,
                        ffnyb[:, tc * HID:(tc + 1) * HID], ALU.bypass, ALU.mult,
                        accum_out=st4["sq"][:, tc:tc + 1])
                return last
            P.stage("vector", f_sq4, reads=("ffnyb",), writes=("sqb", "st_l4"))
            ln_tiny(st4, "st_l4", EPS1)
            ob, obr = outb[ti % 2], f"outb{ti % 2}"

            def f_a4(v):
                last = None
                for tc in range(SUB):
                    last = v.tensor_scalar(ffnyb[:, tc * HID:(tc + 1) * HID],
                                           ffnyb[:, tc * HID:(tc + 1) * HID],
                                           st4["negm"][:, tc:tc + 1], st4["is"][:, tc:tc + 1],
                                           ALU.add, ALU.mult)
                return last
            P.stage("vector", f_a4, reads=("ffnyb", "st_l4"), writes=("ffnyb",))
            P.stage("vector", lambda v: v.tensor_tensor(ob[:], ffnyb[:], x2b[slot][:], ALU.add),
                    reads=("ffnyb", f"x2b{slot}"), writes=(obr,))

            def f_out(sy, ti=ti, ob=ob):
                src = ob.ap().rearrange("p (s f) -> p s f", s=SUB)
                return sy.dma_start(y_t[ti], src)
            P.stage("sync", f_out, reads=(obr,), inc=(f"OU{ti % 2}", 16))

        def exhaust(g):
            for _ in g:
                pass

        for ti in range(n_pre):
            for _ in front_gen(ti, False):
                if do_prepass:
                    prepass_step()
                    prepass_step()
        if do_prepass:
            while pp_i[0] < 2 * NSLAB:
                prepass_step()

        P.stage("vector", lambda v: v.tensor_scalar(carry[:], carry[:], pmaskb[:, 0:1], None,
                                                    ALU.mult),
                reads=("carry", "pmaskb"), writes=("carry",))

        bufmap = {}

        dbg_res = {"x2b": "x2b1", "l1sum": "st_l1", "l1sq": "st_l1", "l1negm": "st_l1", "l1var": "st_l1",
                   "l1is": "st_l1", "lisum": "st_li", "liis": "st_li", "l4is": "st_l4"}

        def emit_dbg():
            for nm, shape, dt in dbg_specs:
                buf = bufmap[nm]
                def f_d(sy, nm=nm, buf=buf):
                    return sy.dma_start(dbg_d[nm], buf[:])
                P.stage("sync", f_d, reads=(dbg_res.get(nm, nm),), inc=("OU0", 16))

        for ti in range(n_own):
            fg = front_gen(ti, True)
            if ti == 0:
                exhaust(fg)
            else:
                zg = ffn_slab_gen(ti - 1)
                done_f = done_z = False
                while not (done_f and done_z):
                    if not done_z:
                        done_z = next(zg, "END") == "END"
                    if not done_f:
                        done_f = next(fg, "END") == "END"
                ffn_tail(ti - 1)
                if debug and ti == 1:
                    bufmap.update(dict(xTb=xTa, qf=qf, kf=kf, vf=vf, attnb=attnb, qku=qku,
                                       innb=innb, yb=yb, x2b=x2b[1], ln1b=ln1b, ffnyb=ffnyb,
                                       carry=carry, gam=gam,
                                       l1sum=stats["l1"]["sum"], l1sq=stats["l1"]["sq"],
                                       l1negm=stats["l1"]["negm"], l1var=stats["l1"]["var"],
                                       l1is=stats["l1"]["is"], lisum=stats["li"]["sum"],
                                       liis=stats["li"]["is"], l4is=stats["l4"]["is"],
                                       gt=gt, stf=stf))
                    emit_dbg()
        if n_own > 0:
            exhaust(ffn_slab_gen(n_own - 1))
            ffn_tail(n_own - 1)

        P.barrier_wait("sync", [("OU0", P.cnt.get("OU0", 0)), ("OU1", P.cnt.get("OU1", 0))])

        with nc.Block() as block:
            def mk(eng_name):
                drain_eng = eng_name in ("vector", "scalar")

                def body(eng):
                    for waits, fn, inc in P.stream[eng_name]:
                        for smn, val in waits:
                            eng.wait_ge(sems[smn], val)
                        if fn is None:
                            continue
                        last = fn(eng)
                        smn, amt = inc
                        if drain_eng and smn in ("V", "S"):
                            last = eng.drain()
                        last.then_inc(sems[smn], amt)
                return body

            block.tensor(mk("tensor"))
            block.vector(mk("vector"))
            block.scalar(mk("scalar"))
            block.gpsimd(mk("gpsimd"))
            block.sync(mk("sync"))

    return nc


_NC_CACHE = None


def kernel(**inputs):
    global _NC_CACHE
    x = np.asarray(inputs["L0_attn_x"], dtype=np.float32)
    Wq = np.asarray(inputs["L0_attn_Wq"], dtype=np.float32)
    bq = np.asarray(inputs["L0_attn_bq"], dtype=np.float32)
    Wk = np.asarray(inputs["L0_attn_Wk"], dtype=np.float32)
    bk = np.asarray(inputs["L0_attn_bk"], dtype=np.float32)
    Wv = np.asarray(inputs["L0_attn_Wv"], dtype=np.float32)
    bv = np.asarray(inputs["L0_attn_bv"], dtype=np.float32)
    Wo = np.asarray(inputs["L0_attn_Wo"], dtype=np.float32)
    bo = np.asarray(inputs["L0_attn_bo"], dtype=np.float32)
    alpha = np.asarray(inputs["L0_attn_alpha"], dtype=np.float32)
    augp = np.asarray(inputs["L0_attn_aug_param"], dtype=np.float32)
    lns = np.asarray(inputs["L0_attn_ln_scale"], dtype=np.float32)
    lnb = np.asarray(inputs["L0_attn_ln_bias"], dtype=np.float32)
    rm = np.asarray(inputs["L0_ffn_rail_maker"], dtype=np.float32)
    up = np.asarray(inputs["L0_ffn_rail_up_proj"], dtype=np.float32)
    wc = np.asarray(inputs["L0_ffn_w_cproj"], dtype=np.float32)

    wq_h = np.ascontiguousarray(Wq.reshape(NFC, 128, NOC, 128).transpose(1, 0, 2, 3)
                                .reshape(128, NFC * NOC * 128))
    wk_h = np.ascontiguousarray(Wk.reshape(NFC, 128, NOC, 128).transpose(1, 0, 2, 3)
                                .reshape(128, NFC * NOC * 128))
    wv_h = np.ascontiguousarray(Wv.reshape(NFC, 128, NOC, 128).transpose(1, 0, 2, 3)
                                .reshape(128, NFC * NOC * 128))
    wo_h = np.ascontiguousarray(Wo.reshape(NOC, 128, HID).transpose(1, 0, 2)
                                .reshape(128, NOC * HID))
    wt_h = np.ascontiguousarray(wc.T)

    rmbd = np.zeros((32, 128, 128), np.float32)
    upbd = np.zeros((32, 128, 512), np.float32)
    for h in range(4):
        for rc in range(8):
            j = h * 8 + rc
            for r8 in range(8):
                r = rc * 8 + r8
                rmbd[j, r8 * 16:(r8 + 1) * 16, r8 * 16:(r8 + 1) * 16] = rm[h, r].T
                upbd[j, r8 * 16:(r8 + 1) * 16, r8 * 64:(r8 + 1) * 64] = up[h, r].T

    heads_of = (np.arange(INNER) // HD)
    sel_h = np.zeros((NOC, 128, 8), np.float32)
    selt_h = np.zeros((NOC, 8, 128), np.float32)
    for oc in range(NOC):
        for p in range(128):
            hh = heads_of[oc * 128 + p]
            sel_h[oc, p, hh] = float(HD) ** -0.5
            selt_h[oc, hh, p] = 1.0
    sel_h = np.ascontiguousarray(sel_h.transpose(1, 0, 2).reshape(128, NOC * 8))
    selt_h = np.ascontiguousarray(selt_h.transpose(1, 0, 2).reshape(8, NOC * 128))

    col = lambda a: np.ascontiguousarray(a.reshape(NOC, 128).T)
    shared = {
        "wq": wq_h, "wk": wk_h, "wv": wv_h, "wo": wo_h, "wt": wt_h,
        "rmbd": rmbd, "upbd": upbd, "sel": sel_h, "selt": selt_h,
        "avg8": np.full((8, 8), -0.125, np.float32), "one88": np.ones((8, 8), np.float32),
        "lnsc": np.ascontiguousarray(np.broadcast_to(lns, (128, HID))),
        "lnbi": np.ascontiguousarray(np.broadcast_to(lnb, (128, HID))),
        "bo": np.ascontiguousarray(np.broadcast_to(bo, (128, HID))),
        "bq": col(bq), "bk": col(bk), "bv": col(bv),
        "cvec": np.ascontiguousarray(np.broadcast_to(
            np.array([EPS1, EPSI, -0.5, -1.0, 0.5, 1.0], np.float32), (128, 6))),
        "alphac": col(alpha.reshape(INNER)), "augpc": col(augp.reshape(INNER)),
    }

    zeros_pre = np.zeros((TH, HID), np.float32)
    ones_m = np.ones((128, 1), np.float32)
    zeros_m = np.zeros((128, 1), np.float32)

    in_maps = []
    for c in range(8):
        b, hf = c // 2, c % 2
        m = dict(shared)
        m["x_own"] = np.ascontiguousarray(x[b, hf * TH:(hf + 1) * TH])
        m["x_pre"] = np.ascontiguousarray(x[b, 0:TH]) if hf else zeros_pre
        m["pmask"] = ones_m if hf else zeros_m
        in_maps.append(m)

    if _NC_CACHE is None:
        _NC_CACHE = build_nc()
    res = run_bass_kernel_spmd(_NC_CACHE, in_maps, core_ids=list(range(8)))

    out = np.empty((NB, SEQ, HID), np.float32)
    for c in range(8):
        b, hf = c // 2, c % 2
        out[b, hf * TH:(hf + 1) * TH] = res.results[c]["y"]
    return out


if __name__ == "__main__":
    print("building...")
    nc = build_nc()
    n_inst = 0
    try:
        n_inst = len(list(nc.all_instructions()))
    except Exception as e:
        print("count failed:", e)
    print("built, instructions:", n_inst)
